# version 05
# baseline (speedup 1.0000x reference)
import numpy as np
import ml_dtypes
from contextlib import ExitStack

import concourse.bass as bass
import concourse.mybir as mybir
from concourse.bass_utils import run_bass_kernel_spmd

F32 = mybir.dt.float32
BF16 = mybir.dt.bfloat16
AF = mybir.ActivationFunctionType
ALU = mybir.AluOpType
AX = mybir.AxisListType

D = 1024
DI = 2048
L = 4096
NST = 16
RK = 64
DFF = 4096
LH = L // 2
NB = 4
NHEAD = 16
HS = 64
NEXP = 8
HID = 2816
BF = ml_dtypes.bfloat16

ENGS = ("sync", "scalar", "vector", "tensor", "gpsimd")


class Sched:

    SERIAL = ("vector", "scalar", "gpsimd")

    def __init__(self, nc, ctx, paranoid=False):
        self.nc = nc
        self.sems = {}
        for e in ENGS:
            self.sems[(e, False)] = ctx.enter_context(nc.semaphore(f"S_{e}"))
            self.sems[(e, True)] = ctx.enter_context(nc.semaphore(f"S_{e}_d"))
        self.ops = {e: [] for e in ENGS}
        self.counts = {(e, c): 0 for e in ENGS for c in (False, True)}
        self.nphase = 0
        self.last_comp = {}
        self.paranoid = paranoid
        self.last_any = None
        self.serialize_eng = set()
        self.last_eng = {}

    @staticmethod
    def _skip_dep(pe, prec, eng, rec):
        if pe != eng:
            return False
        if pe == "tensor":
            return True
        if prec["dma"] and rec["dma"]:
            return True
        return False

    def add(self, eng, emit_fn, deps=(), dma=False, inc=None):
        deps = [d for d in deps if d is not None]
        if self.paranoid and self.last_any is not None:
            deps.append(self.last_any)
        if eng in self.serialize_eng and eng in self.last_eng:
            deps.append(self.last_eng[eng])
        if eng in self.SERIAL and not dma and eng in self.last_comp:
            deps.append(self.last_comp[eng])
        rec = {"emit": emit_fn, "deps": deps, "inc": inc,
               "dma": dma, "eng": eng, "tagged": dma, "count_after": None}
        self.ops[eng].append(rec)
        tok = (eng, rec)
        if eng in self.SERIAL and not dma:
            self.last_comp[eng] = tok
        self.last_any = tok
        self.last_eng[eng] = tok
        return tok

    def dma(self, eng, out, in_, deps=(), slow=False):
        if slow:
            return self.add(eng, lambda q: q.dma_start(
                out=out, in_=in_, allow_slow_non_contiguous=True), deps=deps, dma=True)
        return self.add(eng, lambda q: q.dma_start(out=out, in_=in_),
                        deps=deps, dma=True)

    def set_paranoid(self, flag):
        self.paranoid = flag

    def emit_phase(self, name):
        nc = self.nc
        for e in ENGS:
            for rec in self.ops[e]:
                for (pe, prec) in rec["deps"]:
                    if self._skip_dep(pe, prec, e, rec):
                        continue
                    prec["tagged"] = True
        for e in ENGS:
            cc = self.counts[(e, False)]
            cd = self.counts[(e, True)]
            for rec in self.ops[e]:
                if rec["tagged"]:
                    if rec["dma"]:
                        cd += 16
                    else:
                        cc += 1
                rec["count_after"] = cd if rec["dma"] else cc
            self.counts[(e, False)] = cc
            self.counts[(e, True)] = cd
        pending = {e: self.ops[e] for e in ENGS}
        self.ops = {e: [] for e in ENGS}
        sems = self.sems
        ptr = {e: 0 for e in ENGS}
        done_sem = {}
        for e in ENGS:
            for c in (False, True):
                done_sem[(e, c)] = self.counts[(e, c)] - sum(
                    (16 if r["dma"] else 1) for r in pending[e]
                    if r["tagged"] and r["dma"] == c)
        progress = True
        while progress:
            progress = False
            for e in ENGS:
                while ptr[e] < len(pending[e]):
                    rec = pending[e][ptr[e]]
                    ok = True
                    for (pe, prec) in rec["deps"]:
                        if pe == e:
                            continue
                        if prec["count_after"] is not None and \
                                done_sem.get((pe, prec["dma"]), 0) < prec["count_after"]:
                            ok = False
                            break
                    if not ok:
                        break
                    if rec["tagged"]:
                        done_sem[(e, rec["dma"])] += 16 if rec["dma"] else 1
                    ptr[e] += 1
                    progress = True
        stuck = {e: (ptr[e], len(pending[e])) for e in ENGS if ptr[e] < len(pending[e])}
        if stuck:
            for e, (p, n) in stuck.items():
                rec = pending[e][p]
                info = []
                for (pe, prec) in rec["deps"]:
                    info.append((pe, prec["dma"], prec["count_after"],
                                 done_sem.get((pe, prec["dma"]))))
                print(f"DEADLOCK phase={name} eng={e} op#{p}/{n} deps={info}")
            raise RuntimeError(f"phase {name}: deadlock {stuck}")

        with nc.Block(f"ph{self.nphase}_{name}") as blk:
            for e in ENGS:
                if not pending[e]:
                    continue

                def body(eng_obj, recs=pending[e], ename=e):
                    hiwater = {}
                    for rec in recs:
                        for (pe, prec) in rec["deps"]:
                            if Sched._skip_dep(pe, prec, ename, rec):
                                continue
                            if prec["count_after"] is None:
                                continue
                            key = (pe, prec["dma"])
                            v = prec["count_after"]
                            if hiwater.get(key, -1) < v:
                                eng_obj.wait_ge(sems[key], v)
                                hiwater[key] = v
                        ins = rec["emit"](eng_obj)
                        if rec["tagged"]:
                            if rec.get("inc") == "bare":
                                ins.then_inc(sems[(ename, rec["dma"])])
                            else:
                                ins.then_inc(sems[(ename, rec["dma"])],
                                             16 if rec["dma"] else 1)

                getattr(blk, e)(body)
        self.nphase += 1



def emit_A(nc, S, ctx, sbt, psA, psB, io, cm):
    x = io["x"]
    winT = io["winT"]
    wxT = io["wxT"]
    wdtT = io["wdtT"]
    convw = io["convw"]
    convb = io["convb"]
    alog = io["alog"]
    bdt = io["bdt"]
    dvec = io["dvec"]
    woutT = io["woutT"]
    wupT = io["wupT"]
    wdnT = io["wdnT"]
    idbf = io["idbf"]
    idf32 = io["idf32"]
    jrevh = io["jrevh"]
    SEND = io["SEND"]
    xssm_d = io["xssm_d"]
    sz_d = io["sz_d"]
    v_d = io["v_d"]
    bc_d = io["bc_d"]
    sq_d = io["sq_d"]
    h_d = io["h_d"]

    identbf = cm["identbf"]
    identf = cm["identf"]
    onesf_row = cm["onesf_row"]
    onesf_col = cm["onesf_col"]
    onesbf_col = cm["onesbf_col"]

    if True:
        pA_outer = ExitStack()
        consts = sbt("consts", [128, 384], F32, pA_outer)
        carry = sbt("carry", [128, 256], F32, pA_outer)
        dtr_bf = sbt("dtr_bf", [64, L], BF16, pA_outer)
        p0s = ExitStack()
        Abuf = sbt("Abuf", [128, 256], F32, p0s)

        t_cw = S.dma("sync", consts[:, 0:64], convw.rearrange("(t p) k -> p t k", p=128))
        t_cb = S.dma("sync", consts[:, 64:80], convb.rearrange("(t p) o -> p t o", p=128), slow=True)
        t_bd = S.dma("sync", consts[:, 80:96], bdt.rearrange("(t p) o -> p t o", p=128), slow=True)
        t_dv = S.dma("sync", consts[:, 96:112], dvec.rearrange("(t p) o -> p t o", p=128), slow=True)
        t_al = S.dma("sync", consts[:, 128:384], alog.rearrange("(t p) n -> p t n", p=128))
        t_ib = S.dma("sync", identbf[:], idbf[:, :])
        t_if = S.dma("sync", identf[:], idf32[:, :])
        t_o1 = S.add("gpsimd", lambda g: g.memset(onesf_row[:], 1.0))
        t_o2 = S.add("gpsimd", lambda g: g.memset(onesf_col[:], 1.0))
        t_o3 = S.add("gpsimd", lambda g: g.memset(onesbf_col[:], 1.0))
        t_o4 = S.add("gpsimd", lambda g: g.memset(carry[:], 0.0))
        t_ae = S.add("scalar", lambda s: s.activation(Abuf[:], consts[:, 128:384], AF.Exp),
                     deps=[t_al])
        t_an = S.add("vector", lambda v: v.tensor_scalar_mul(consts[:, 128:384], Abuf[:], -1.0),
                     deps=[t_ae])
        S.emit_phase("consts")
        p0s.close()

        p1 = ExitStack()
        p1x = ExitStack()
        xdbl = sbt("xdbl", [96, L], F32, p1)
        sumsq_row = sbt("sumsq_row", [1, LH], F32, p1)
        rcp_row = sumsq_row
        rcp2_row = sbt("rcp2_row", [1, LH], F32, p1)
        hss_row = sbt("hss_row", [1, LH], F32, p1)
        rms_row = hss_row
        xnT = sbt("xnT", [128, 8 * L], BF16, p1x, side="right")
        p1a = ExitStack()
        xt = [sbt(f"xt{j}", [128, D], F32, p1a) for j in range(2)]
        xsq = [sbt(f"xsq{j}", [128, D], F32, p1a) for j in range(2)]
        xnt = sbt("xnt", [128, D], BF16, p1a)
        stat = sbt("stat", [128, 32], F32, p1a)
        sqcols = sbt("sqcols", [128, 32], F32, p1a)
        sqtr = sbt("sqtr", [32, 128], F32, p1a)

        tok_norm = [None] * 32
        tok_tr_last = [None] * 32
        tok_rsq = [None] * 32
        tok_cp = {}
        tffn = None
        for i in range(32):
            sb = (i % 2) * 16
            deps = [tok_norm[i - 2]] if i >= 2 else []
            tl = S.dma("sync", xt[i % 2][:], x[i * 128:(i + 1) * 128, :], deps=deps)
            tsq = S.add("scalar", lambda s, i=i: s.activation(xsq[i % 2][:], xt[i % 2][:],
                                                              AF.Square),
                        deps=[tl] + ([tok_rsq[i - 2]] if i >= 2 else []))
            S.add("vector", lambda v, i=i, sb=sb: v.reduce_sum(
                stat[:, sb + 0:sb + 1], xt[i % 2][:], axis=AX.X), deps=[tl])
            tok_rsq[i] = S.add("vector", lambda v, i=i, sb=sb: v.reduce_sum(
                stat[:, sb + 1:sb + 2], xsq[i % 2][:], axis=AX.X), deps=[tsq])
            S.add("vector", lambda v, sb=sb: v.tensor_scalar_mul(
                stat[:, sb + 2:sb + 3], stat[:, sb + 0:sb + 1], 1.0 / D))
            S.add("vector", lambda v, sb=sb: v.tensor_scalar_mul(
                stat[:, sb + 3:sb + 4], stat[:, sb + 1:sb + 2], 1.0 / D))
            S.add("vector", lambda v, sb=sb: v.tensor_mul(
                stat[:, sb + 4:sb + 5], stat[:, sb + 2:sb + 3], stat[:, sb + 2:sb + 3]))
            S.add("vector", lambda v, sb=sb: v.tensor_sub(
                stat[:, sb + 5:sb + 6], stat[:, sb + 3:sb + 4], stat[:, sb + 4:sb + 5]))
            tvp = S.add("vector", lambda v, sb=sb: v.tensor_scalar_add(
                stat[:, sb + 6:sb + 7], stat[:, sb + 5:sb + 6], 1e-5))
            tsqr = S.add("scalar", lambda s, sb=sb: s.activation(
                stat[:, sb + 7:sb + 8], stat[:, sb + 6:sb + 7], AF.Sqrt), deps=[tvp])
            trs = S.add("vector", lambda v, sb=sb: v.reciprocal(
                stat[:, sb + 8:sb + 9], stat[:, sb + 7:sb + 8]), deps=[tsqr])
            tnb = S.add("vector", lambda v, sb=sb: v.tensor_scalar(
                stat[:, sb + 9:sb + 10], stat[:, sb + 2:sb + 3], stat[:, sb + 8:sb + 9],
                -1.0, ALU.mult, ALU.mult))
            ndeps = [tnb]
            if i >= 1:
                ndeps.append(tok_tr_last[i - 1])
            tok_norm[i] = S.add("scalar", lambda s, i=i, sb=sb: s.activation(
                xnt[:], xt[i % 2][:], AF.Identity, bias=stat[:, sb + 9:sb + 10],
                scale=stat[:, sb + 8:sb + 9]), deps=ndeps)
            S.add("vector", lambda v, sb=sb: v.tensor_mul(
                stat[:, sb + 10:sb + 11], stat[:, sb + 5:sb + 6], stat[:, sb + 8:sb + 9]))
            S.add("vector", lambda v, sb=sb: v.tensor_mul(
                stat[:, sb + 11:sb + 12], stat[:, sb + 10:sb + 11], stat[:, sb + 8:sb + 9]))
            tffn = S.add("vector", lambda v, i=i, sb=sb: v.tensor_scalar_mul(
                sqcols[:, i:i + 1], stat[:, sb + 11:sb + 12], float(D)))
            for k in range(8):
                war = [tok_cp[i * 8 + k - 2]] if (i * 8 + k) >= 2 else []
                ttr = S.add("tensor", lambda t, i=i, k=k: t.transpose(
                    psB[:, (k % 2) * 1024:(k % 2) * 1024 + 128],
                    xnt[:, k * 128:(k + 1) * 128], identbf[:]),
                    deps=[tok_norm[i], t_ib] + war)
                tok_cp[i * 8 + k] = S.add("vector", lambda v, i=i, k=k: v.tensor_copy(
                    xnT[:, k * L + i * 128: k * L + (i + 1) * 128],
                    psB[:, (k % 2) * 1024:(k % 2) * 1024 + 128]), deps=[ttr])
                tok_tr_last[i] = ttr
        tfl = S.add("tensor", lambda t: t.transpose(psA[0:32, 0:128], sqcols[:, 0:32],
                                                    identf[:]), deps=[tffn, t_if])
        tfc = S.add("vector", lambda v: v.tensor_copy(sqtr[:], psA[0:32, 0:128]), deps=[tfl])
        tq1 = S.dma("gpsimd", sq_d[:, :], sqtr[:], deps=[tfc])
        tq2 = S.dma("gpsimd", sumsq_row[0:1, :],
                    sq_d[0:16, :].rearrange("t p -> (t p)").unsqueeze(0), deps=[tq1])
        S.emit_phase("ln")
        p1a.close()

        wtern = sbt("wtern", [128, 32768], BF16, p1)
        p2w = ExitStack()
        wx_bf = sbt("wx_bf", [128, 16 * 96], BF16, p2w)
        window = sbt("window", [128, 515], F32, p2w)
        tails = sbt("tails", [128, 48], F32, p2w)
        acc = [sbt(f"acc{j}", [128, 512], F32, p2w) for j in range(2)]
        xs_chunk = sbt("xs_chunk", [128, 16 * 512], BF16, p2w)
        sz_t = [sbt(f"sz_t{j}", [128, 512], BF16, p2w) for j in range(2)]

        tcast = []
        for kt in range(8):
            tcast.append(S.dma("gpsimd", wtern[:, kt * 4096:(kt + 1) * 4096],
                               winT[kt * 128:(kt + 1) * 128, :]))
        tcwx = S.dma("gpsimd", wx_bf[:], wxT.rearrange("(t p) e -> p t e", p=128))
        tz = S.add("vector", lambda v: v.memset(tails[:], 0.0))

        tok_mm = {}
        tok_evx = {}
        tok_evz = {}
        tok_silu = {}
        tok_wr = {}
        tok_zwr = {}
        tok_convd = {}
        tok_xde = None
        prev_silu = None
        for ch in range(8):
            for mt in range(32):
                slot = (mt % 2) * 512
                gi = ch * 32 + mt
                war = []
                if gi >= 2:
                    pch, pmt = divmod(gi - 2, 32)
                    war = [tok_evx[(pch, pmt)] if pmt < 16 else tok_evz[(pch, pmt - 16)]]

                def mmfn(t, ch=ch, mt=mt, slot=slot):
                    for kt in range(8):
                        ins = t.matmul(psA[:, slot:slot + 512],
                                       wtern[:, kt * 4096 + mt * 128:kt * 4096 + mt * 128 + 128],
                                       xnT[:, kt * L + ch * 512:kt * L + (ch + 1) * 512],
                                       start=(kt == 0), stop=(kt == 7))
                    return ins
                tok_mm[(ch, mt)] = S.add("tensor", mmfn,
                                         deps=[tcast[7], tok_cp[255]] + war)
                if mt < 16:
                    tt = mt
                    tin_deps = [tz] if ch == 0 else [tok_convd[(ch - 1, tt)]]
                    S.add("vector", lambda v, tt=tt: v.tensor_copy(
                        window[:, 0:3], tails[:, tt * 3:tt * 3 + 3]), deps=tin_deps)
                    tok_evx[(ch, tt)] = S.add("vector", lambda v, slot=slot: v.tensor_copy(
                        window[:, 3:515], psA[:, slot:slot + 512]), deps=[tok_mm[(ch, mt)]])
                    S.add("vector", lambda v, tt=tt: v.tensor_copy(
                        tails[:, tt * 3:tt * 3 + 3], window[:, 512:515]))
                    c0_deps = [t_cw] + ([prev_silu] if prev_silu else [])
                    S.add("vector", lambda v, tt=tt: v.tensor_scalar_mul(
                        acc[0][:], window[:, 0:512], consts[:, 4 * tt:4 * tt + 1]),
                        deps=c0_deps)
                    S.add("vector", lambda v, tt=tt: v.scalar_tensor_tensor(
                        acc[1][:], window[:, 1:513], consts[:, 4 * tt + 1:4 * tt + 2],
                        acc[0][:], ALU.mult, ALU.add))
                    S.add("vector", lambda v, tt=tt: v.scalar_tensor_tensor(
                        acc[0][:], window[:, 2:514], consts[:, 4 * tt + 2:4 * tt + 3],
                        acc[1][:], ALU.mult, ALU.add))
                    tok_convd[(ch, tt)] = S.add("vector", lambda v, tt=tt: v.scalar_tensor_tensor(
                        acc[1][:], window[:, 3:515], consts[:, 4 * tt + 3:4 * tt + 4],
                        acc[0][:], ALU.mult, ALU.add))
                    sdeps = [tok_convd[(ch, tt)], t_cb]
                    if ch > 0:
                        sdeps.append(tok_wr[(ch - 1, tt)])
                    tok_silu[(ch, tt)] = S.add("scalar", lambda s, tt=tt: s.activation(
                        xs_chunk[:, tt * 512:(tt + 1) * 512], acc[1][:], AF.Silu,
                        bias=consts[:, 64 + tt:64 + tt + 1]), deps=sdeps)
                    prev_silu = tok_silu[(ch, tt)]
                    tok_wr[(ch, tt)] = S.dma(
                        "sync", xssm_d[tt * 128:(tt + 1) * 128, ch * 512:(ch + 1) * 512],
                        xs_chunk[:, tt * 512:(tt + 1) * 512], deps=[tok_silu[(ch, tt)]])
                else:
                    tzi = mt - 16
                    zslot = tzi % 2
                    gzi = ch * 16 + tzi
                    zdeps = [tok_mm[(ch, mt)]]
                    if gzi >= 2:
                        pch, ptz = divmod(gzi - 2, 16)
                        zdeps.append(tok_zwr[(pch, ptz)])
                    tok_evz[(ch, tzi)] = S.add("scalar", lambda s, zslot=zslot, slot=slot:
                                               s.activation(sz_t[zslot][:],
                                                            psA[:, slot:slot + 512], AF.Silu),
                                               deps=zdeps)
                    tok_zwr[(ch, tzi)] = S.dma(
                        "scalar", sz_d[tzi * 128:(tzi + 1) * 128, ch * 512:(ch + 1) * 512],
                        sz_t[zslot][:], deps=[tok_evz[(ch, tzi)]])

            def xdblfn(t, ch=ch):
                for tt in range(16):
                    ins = t.matmul(psA[0:96, 1024:1536], wx_bf[:, tt * 96:(tt + 1) * 96],
                                   xs_chunk[:, tt * 512:(tt + 1) * 512],
                                   start=(tt == 0), stop=(tt == 15))
                return ins
            xd_deps = [tcwx] + [tok_silu[(ch, tt)] for tt in range(16)]
            if tok_xde is not None:
                xd_deps.append(tok_xde)
            tok_xd = S.add("tensor", xdblfn, deps=xd_deps)
            tok_xde = S.add("vector", lambda v, ch=ch: v.tensor_copy(
                xdbl[:, ch * 512:(ch + 1) * 512], psA[0:96, 1024:1536]), deps=[tok_xd])
        tbc = S.dma("gpsimd", bc_d[:, :], xdbl[64:96, :], deps=[tok_xde])
        S.emit_phase("xz")
        p2w.close()

        stage = [sbt(f"stage{j}", [128, 2048], F32, p1) for j in range(2)]
        abscols = sbt("abscols", [128, 34], F32, p1)
        trow = sbt("trow", [1, 2], F32, p1)
        thr_col = sbt("thr_col", [128, 2], F32, p1)
        posb = sbt("posb", [128, 512], F32, p1)
        negb = sbt("negb", [128, 512], F32, p1)
        p3w = ExitStack()
        rbc = sbt("rbc", [128, 512], F32, p3w)
        tfp = [sbt("tfp0", [128, 512], F32, p3w)] * 2
        hbuf = [sbt(f"hbuf{j}", [128, 512], BF16, p3w) for j in range(2)]

        def ternarize(srcslice_fn, npiece, pcols, wslice_fn, nelem):
            tred = []
            for kp in range(npiece):
                deps = [tred[kp - 2]] if kp >= 2 else []
                tl = S.dma("sync", stage[kp % 2][:, 0:pcols], srcslice_fn(kp), deps=deps)
                tred.append(S.add("vector", lambda v, kp=kp: v.reduce_sum(
                    abscols[:, kp:kp + 1], stage[kp % 2][:, 0:pcols], axis=AX.X,
                    apply_absolute_value=True), deps=[tl]))
            tacc = S.add("vector", lambda v: v.tensor_add(
                abscols[:, 32:33], abscols[:, 0:1], abscols[:, 1:2]), deps=[tred[1]])
            for j in range(2, npiece):
                tacc = S.add("vector", lambda v, j=j: v.tensor_add(
                    abscols[:, 32:33], abscols[:, 32:33], abscols[:, j:j + 1]),
                    deps=[tred[j]])
            tmm1 = S.add("tensor", lambda t: t.matmul(
                psA[0:1, 1536:1537], onesf_col[:, 0:1], abscols[:, 32:33],
                start=True, stop=True), deps=[tacc, t_o2])
            S.add("vector", lambda v: v.tensor_copy(trow[0:1, 0:1], psA[0:1, 1536:1537]),
                  deps=[tmm1])
            tth = S.add("vector", lambda v: v.tensor_scalar_mul(
                trow[0:1, 1:2], trow[0:1, 0:1], 0.5 / nelem))
            tmm2 = S.add("tensor", lambda t: t.matmul(
                psA[:, 1792:1793], onesf_row[0:1, :], trow[0:1, 1:2],
                start=True, stop=True), deps=[tth, t_o1])
            S.add("vector", lambda v: v.tensor_copy(thr_col[:, 0:1], psA[:, 1792:1793]),
                  deps=[tmm2])
            tng = S.add("vector", lambda v: v.tensor_scalar_mul(
                thr_col[:, 1:2], thr_col[:, 0:1], -1.0))
            tw = []
            for kp in range(npiece):
                deps = [tng] + ([tw[kp - 2]] if kp >= 2 else [])
                tl = S.dma("sync", stage[kp % 2][:, 0:pcols], srcslice_fn(kp), deps=deps)
                tlast = None
                for so in range(0, pcols, 512):
                    w = min(512, pcols - so)
                    S.add("vector", lambda v, kp=kp, so=so, w=w: v.tensor_single_scalar(
                        posb[:, 0:w], stage[kp % 2][:, so:so + w], thr_col[:, 0:1],
                        ALU.is_gt), deps=[tl])
                    S.add("vector", lambda v, kp=kp, so=so, w=w: v.tensor_single_scalar(
                        negb[:, 0:w], stage[kp % 2][:, so:so + w], thr_col[:, 1:2],
                        ALU.is_lt))
                    tlast = S.add("vector", lambda v, kp=kp, so=so, w=w: v.tensor_sub(
                        wslice_fn(kp)[:, so:so + w], posb[:, 0:w], negb[:, 0:w]))
                tw.append(tlast)
            return tw

        twup = ternarize(
            lambda kp: wupT[(kp // 2) * 128:(kp // 2 + 1) * 128,
                            (kp % 2) * 2048:(kp % 2 + 1) * 2048],
            16, 2048,
            lambda kp: wtern[:, (kp // 2) * 4096 + (kp % 2) * 2048:
                             (kp // 2) * 4096 + (kp % 2 + 1) * 2048],
            1024.0 * 4096.0)

        tsqx = S.add("scalar", lambda s: s.activation(
            rms_row[0:1, :], sumsq_row[0:1, :], AF.Sqrt, scale=1.0 / D), deps=[tq2])
        S.add("vector", lambda v: v.tensor_scalar_add(rms_row[0:1, :], rms_row[0:1, :],
                                                      1e-8), deps=[tsqx])
        trcp = S.add("vector", lambda v: v.reciprocal(rcp_row[0:1, :], rms_row[0:1, :]))

        tok_gelu = {}
        tok_t1 = {}
        tok_hw = {}
        trbe = None
        for ch in range(4):
            trb = S.add("tensor", lambda t, ch=ch: t.matmul(
                psA[:, 2048:2560], onesf_row[0:1, :], rcp_row[0:1, ch * 512:(ch + 1) * 512],
                start=True, stop=True), deps=[trcp, t_o1, trbe])
            trbe = S.add("vector", lambda v: v.tensor_copy(rbc[:], psA[:, 2048:2560]),
                         deps=[trb])
            for mt in range(32):
                gi = ch * 32 + mt
                war = [tok_t1[gi - 2]] if gi >= 2 else []

                def upfn(t, ch=ch, mt=mt):
                    for kt in range(8):
                        ins = t.matmul(psA[:, (mt % 2) * 512:(mt % 2) * 512 + 512],
                                       wtern[:, kt * 4096 + mt * 128:kt * 4096 + mt * 128 + 128],
                                       xnT[:, kt * L + ch * 512:kt * L + (ch + 1) * 512],
                                       start=(kt == 0), stop=(kt == 7))
                    return ins
                tmm = S.add("tensor", upfn, deps=[twup[15], trbe] + war)
                t1_deps = [tmm] + ([tok_gelu[gi - 1]] if gi >= 1 else [])
                tok_t1[gi] = S.add("vector", lambda v, mt=mt, gi=gi: v.tensor_mul(
                    tfp[gi % 2][:], psA[:, (mt % 2) * 512:(mt % 2) * 512 + 512], rbc[:]),
                    deps=t1_deps)
                g_deps = [tok_t1[gi]] + ([tok_hw[gi - 2]] if gi >= 2 else [])
                tok_gelu[gi] = S.add("scalar", lambda s, gi=gi: s.activation(
                    hbuf[gi % 2][:], tfp[gi % 2][:], AF.Gelu), deps=g_deps)
                tok_hw[gi] = S.dma("gpsimd",
                                   h_d[mt * 128:(mt + 1) * 128, ch * 512:(ch + 1) * 512],
                                   hbuf[gi % 2][:], deps=[tok_gelu[gi]])
        S.emit_phase("up")
        p1x.close()
        p3w.close()

        p3s = ExitStack()
        hld2 = [sbt(f"hld2_{j}", [128, 512], BF16, p3s) for j in range(2)]
        sq2 = [sbt(f"sq2_{j}", [128, 512], BF16, p3s) for j in range(2)]
        tok_hss = {}
        tsq_prev = {}
        for ch in range(4):
            tmm_prev = None
            for kf in range(32):
                gk = ch * 32 + kf
                ldeps = [tok_hw[gk]]
                if gk >= 2:
                    ldeps.append(tsq_prev[gk - 2])
                tl = S.dma("sync", hld2[kf % 2][:],
                           h_d[kf * 128:(kf + 1) * 128, ch * 512:(ch + 1) * 512], deps=ldeps)
                sdeps = [tl]
                if gk >= 2:
                    sdeps.append(tok_hss.get("mm%d" % (gk - 2)))
                tsq_prev[gk] = S.add("vector", lambda v, kf=kf: v.tensor_mul(
                    sq2[kf % 2][:], hld2[kf % 2][:], hld2[kf % 2][:]), deps=sdeps)
                mmdeps = [tsq_prev[gk], t_o3]
                if kf == 0 and ch > 0:
                    mmdeps.append(tok_hss[ch - 1])
                tmm_prev = S.add("tensor", lambda t, kf=kf: t.matmul(
                    psA[0:1, 2560:3072], onesbf_col[:, 0:1], sq2[kf % 2][:],
                    start=(kf == 0), stop=(kf == 31)), deps=mmdeps)
                tok_hss["mm%d" % gk] = tmm_prev
            tok_hss[ch] = S.add("vector", lambda v, ch=ch: v.tensor_copy(
                hss_row[0:1, ch * 512:(ch + 1) * 512], psA[0:1, 2560:3072]),
                deps=[tmm_prev])
        ths = S.add("scalar", lambda s: s.activation(
            rcp2_row[0:1, :], hss_row[0:1, :], AF.Sqrt, scale=1.0 / DFF),
            deps=[tok_hss[0], tok_hss[1], tok_hss[2], tok_hss[3]])
        S.add("vector", lambda v: v.tensor_scalar_add(rcp2_row[0:1, :], rcp2_row[0:1, :],
                                                      1e-8), deps=[ths])
        tr2 = S.add("vector", lambda v: v.reciprocal(rcp2_row[0:1, :], rcp2_row[0:1, :]))
        S.emit_phase("hss")
        p3s.close()

        hldb = sbt("hldb", [128, 8 * 512], BF16, p1)
        hnb = sbt("hnb", [128, 8 * 512], BF16, p1)
        rbc2 = sbt("rbc2", [128, 512], F32, p1)
        gtile = [sbt(f"gtile{j}", [128, 512], BF16, p1) for j in range(2)]

        twdn = ternarize(
            lambda kp: wdnT[kp * 128:(kp + 1) * 128, :], 32, 1024,
            lambda kp: wtern[:, kp * 1024:(kp + 1) * 1024], 4096.0 * 1024.0)

        tok_hn = {}
        tok_gw = {}
        tok_gs = {}
        tok_dmm = {}
        last_dmm = None
        trbe2 = None
        for ch in range(4):
            trb2 = S.add("tensor", lambda t, ch=ch: t.matmul(
                psA[:, 2048:2560], onesf_row[0:1, :], rcp2_row[0:1, ch * 512:(ch + 1) * 512],
                start=True, stop=True), deps=[tr2, t_o1, trbe2])
            trbe2 = S.add("vector", lambda v: v.tensor_copy(rbc2[:], psA[:, 2048:2560]),
                          deps=[trb2])
            for mtg in range(2):
                for kfh in range(4):
                    for j in range(8):
                        kf = kfh * 8 + j
                        deps = [tok_hw[ch * 32 + kf], trbe2]
                        if last_dmm is not None:
                            deps.append(last_dmm)
                        tl = S.dma("scalar", hldb[:, j * 512:(j + 1) * 512],
                                   h_d[kf * 128:(kf + 1) * 128, ch * 512:(ch + 1) * 512],
                                   deps=deps)
                        tok_hn[j] = S.add("vector", lambda v, j=j: v.tensor_mul(
                            hnb[:, j * 512:(j + 1) * 512], hldb[:, j * 512:(j + 1) * 512],
                            rbc2[:]), deps=[tl])
                    for mi in range(4):
                        mtd = mtg * 4 + mi
                        gi = ch * 8 + mtd
                        deps = [twdn[31]] + [tok_hn[j] for j in range(8)]
                        if kfh == 0:
                            if (ch, mtg) != (0, 0):
                                pch, pmtg = divmod(ch * 2 + mtg - 1, 2)
                                deps.append(tok_gs[pch * 8 + pmtg * 4 + mi])

                        def dnfn(t, mtd=mtd, mi=mi, kfh=kfh):
                            for j in range(8):
                                kf = kfh * 8 + j
                                ins = t.matmul(
                                    psA[:, mi * 512:(mi + 1) * 512],
                                    wtern[:, kf * 1024 + mtd * 128:kf * 1024 + mtd * 128 + 128],
                                    hnb[:, j * 512:(j + 1) * 512],
                                    start=(kfh == 0 and j == 0),
                                    stop=(kfh == 3 and j == 7))
                            return ins
                        tok_dmm[(ch, mtd, kfh)] = S.add("tensor", dnfn, deps=deps)
                    last_dmm = tok_dmm[(ch, mtg * 4 + 3, kfh)]
                for mi in range(4):
                    mtd = mtg * 4 + mi
                    gi = ch * 8 + mtd
                    gs_deps = [tok_dmm[(ch, mtd, 3)]]
                    if gi >= 2:
                        gs_deps.append(tok_gw[gi - 2])
                    tok_gs[gi] = S.add("scalar", lambda s, mi=mi, gi=gi: s.activation(
                        gtile[gi % 2][:], psA[:, mi * 512:(mi + 1) * 512],
                        AF.Silu), deps=gs_deps)
                    tok_gw[gi] = S.dma(
                        "sync", SEND[mtd * 128:(mtd + 1) * 128,
                                     L + ch * 512:L + (ch + 1) * 512],
                        gtile[gi % 2][:], deps=[tok_gs[gi]])
        tcdr = S.add("vector", lambda v: v.tensor_copy(dtr_bf[:], xdbl[0:64, :]),
                     deps=[tbc])
        S.emit_phase("down")

        p1.close()

        p4 = ExitStack()
        bcb = sbt("bcb", [128, 32 * 1024], F32, p4)
        wdt_bf = sbt("wdt_bf", [64, DI], BF16, p4)
        xs4 = [sbt(f"xs4_{j}", [128, 1024], BF16, p4) for j in range(2)]
        sz4 = [sbt(f"sz4_{j}", [128, 1024], BF16, p4) for j in range(2)]
        ub4 = sbt("ub4", [128, 512], F32, p4)
        ab4 = sbt("ab4", [128, 512], F32, p4)
        eb4 = sbt("eb4", [128, 512], F32, p4)
        lb4 = sbt("lb4", [128, 512], F32, p4)
        rl4 = sbt("rl4", [128, 512], F32, p4)
        dtb = sbt("dtb", [128, 1024], F32, p4)
        dtx = sbt("dtx", [128, 1024], F32, p4)
        dab = [sbt(f"dab{j}", [128, 1024], F32, p4) for j in range(2)]
        dbx = [sbt(f"dbx{j}", [128, 1024], F32, p4) for j in range(2)]
        hb4 = [sbt(f"hb4_{j}", [128, 1024], F32, p4) for j in range(2)]
        tmp4 = [sbt(f"tmp4_{j}", [128, 1024], F32, p4) for j in range(2)]
        vt4 = sbt("vt4", [128, 512], F32, p4)
        vb4 = [sbt(f"vb4_{j}", [128, 1024], BF16, p4) for j in range(2)]

        tcdw = S.dma("gpsimd", wdt_bf[:], wdtT[:, :])

        tok_bc = {}
        tok_dbx = {}
        tok_tts = {}
        tok_tmp = {}
        tok_ymm = {}
        tok_vb = {}
        tok_vw = {}
        tok_xl = {}
        tok_zl = {}
        tok_dtx = {}
        tok_u = {}
        tok_dalast = {}
        te1_prev = None
        tdtb_prev = None
        for lq in range(4):
            for n in range(32):
                deps = [tbc]
                if lq > 0:
                    deps += [tok_tmp[(lq - 1, 15, 14)], tok_tmp[(lq - 1, 15, 15)],
                             tok_dbx[(lq - 1, 15, 15)], tok_dbx[(lq - 1, 15, 14)]]
                q = "sync" if n % 2 == 0 else "scalar"
                tok_bc[(lq, n)] = S.dma(
                    q, bcb[:, n * 1024:(n + 1) * 1024],
                    bc_d[n:n + 1, lq * 1024:(lq + 1) * 1024].partition_broadcast(128),
                    deps=deps)
            for t in range(16):
                idx = lq * 16 + t
                ldeps = [tok_vw[idx - 2]] if idx >= 2 else []
                tok_xl[idx] = S.dma("sync", xs4[t % 2][:],
                                    xssm_d[t * 128:(t + 1) * 128, lq * 1024:(lq + 1) * 1024],
                                    deps=ldeps)
                tok_zl[idx] = S.dma("scalar", sz4[t % 2][:],
                                    sz_d[t * 128:(t + 1) * 128, lq * 1024:(lq + 1) * 1024],
                                    deps=ldeps)
                dt_deps = [tcdw, tcdr]
                if idx > 0:
                    dt_deps.append(tok_u[idx - 1])

                def dtfn(t_, lq=lq, t=t):
                    for c in range(2):
                        ins = t_.matmul(psA[:, 2048 + c * 512:2048 + (c + 1) * 512],
                                        wdt_bf[:, t * 128:(t + 1) * 128],
                                        dtr_bf[:, lq * 1024 + c * 512:lq * 1024 + (c + 1) * 512],
                                        start=True, stop=True)
                    return ins
                tdt = S.add("tensor", dtfn, deps=dt_deps)
                u_deps = [tdt, t_bd]
                if idx > 0:
                    u_deps.append(tok_dalast[idx - 1])
                tdtb = None
                for sc in range(2):
                    so = sc * 512
                    ud = list(u_deps) if sc == 0 else []
                    if te1_prev:
                        ud.append(te1_prev)
                    tu = S.add("vector", lambda v, t=t, so=so: v.tensor_scalar_add(
                        ub4[:], psA[:, 2048 + so:2560 + so],
                        consts[:, 80 + t:80 + t + 1]), deps=ud)
                    if sc == 1:
                        tok_u[idx] = tu
                    tab = S.add("scalar", lambda s: s.activation(
                        ab4[:], ub4[:], AF.Abs), deps=[tu])
                    te1 = S.add("scalar", lambda s: s.activation(
                        eb4[:], ab4[:], AF.Exp, scale=-1.0), deps=[tab])
                    te2_deps = [te1] + ([tdtb_prev] if tdtb_prev else [])
                    te2 = S.add("scalar", lambda s: s.activation(
                        lb4[:], eb4[:], AF.Ln, bias=1.0), deps=te2_deps)
                    S.add("vector", lambda v: v.tensor_scalar_max(rl4[:], ub4[:], 0.0),
                          deps=[te1])
                    tdtb = S.add("vector", lambda v, so=so: v.tensor_add(
                        dtb[:, so:so + 512], rl4[:], lb4[:]), deps=[te2])
                    te1_prev, tdtb_prev = te1, tdtb
                dtx_deps = [tok_xl[idx], tdtb]
                if idx > 0:
                    dtx_deps.append(tok_dbx[(divmod(idx - 1, 16)[0],
                                             divmod(idx - 1, 16)[1], 15)])
                tok_dtx[idx] = S.add("vector", lambda v, t=t: v.tensor_mul(
                    dtx[:], dtb[:], xs4[t % 2][:]), deps=dtx_deps)
                for n in range(16):
                    adeps = [tdtb, t_an]
                    if n >= 2:
                        adeps.append(tok_tts[(lq, t, n - 2)])
                    tda = S.add("scalar", lambda s, t=t, n=n: s.activation(
                        dab[n % 2][:], dtb[:], AF.Exp,
                        scale=consts[:, 128 + t * 16 + n:128 + t * 16 + n + 1]),
                        deps=adeps)
                    bdeps = [tok_dtx[idx], tok_bc[(lq, n)]]
                    if n >= 2:
                        bdeps.append(tok_tts[(lq, t, n - 2)])
                    tok_dbx[(lq, t, n)] = S.add("gpsimd", lambda g, n=n: g.tensor_tensor(
                        dbx[n % 2][:], dtx[:], bcb[:, n * 1024:(n + 1) * 1024], ALU.mult),
                        deps=bdeps)
                    init = 0.0 if lq == 0 else carry[:, t * 16 + n:t * 16 + n + 1]
                    hdeps = [tda, tok_dbx[(lq, t, n)]]
                    if n >= 2:
                        hdeps.append(tok_tmp[(lq, t, n - 2)])
                    tok_tts[(lq, t, n)] = S.add(
                        "vector", lambda v, n=n, init=init: v.tensor_tensor_scan(
                            hb4[n % 2][:], dab[n % 2][:], dbx[n % 2][:], init,
                            ALU.mult, ALU.add), deps=hdeps)
                    S.add("vector", lambda v, t=t, n=n: v.tensor_copy(
                        carry[:, t * 16 + n:t * 16 + n + 1], hb4[n % 2][:, 1023:1024]))
                    teng = "gpsimd" if n % 2 == 0 else "vector"
                    tmdeps = [tok_tts[(lq, t, n)], tok_bc[(lq, 16 + n)]]
                    if n >= 2:
                        tmdeps.append(tok_ymm[(lq, t, n - 2)])
                    tok_tmp[(lq, t, n)] = S.add(teng, lambda e, n=n: e.tensor_tensor(
                        tmp4[n % 2][:], hb4[n % 2][:],
                        bcb[:, (16 + n) * 1024:(16 + n + 1) * 1024], ALU.mult),
                        deps=tmdeps)
                    ydeps = [tok_tmp[(lq, t, n)], t_if]
                    if n == 0 and idx > 0:
                        ydeps.append(tok_vb[idx - 1])

                    def yfn(t_, n=n):
                        for c in range(2):
                            ins = t_.matmul(psA[:, c * 512:(c + 1) * 512], identf[:],
                                            tmp4[n % 2][:, c * 512:(c + 1) * 512],
                                            start=(n == 0), stop=(n == 15))
                        return ins
                    tok_ymm[(lq, t, n)] = S.add("tensor", yfn, deps=ydeps)
                tok_dalast[idx] = tda
                tok_vb[idx] = None
                for sc in range(2):
                    so = sc * 512
                    tvt = S.add("vector", lambda v, t=t, so=so: v.scalar_tensor_tensor(
                        vt4[:], xs4[t % 2][:, so:so + 512], consts[:, 96 + t:96 + t + 1],
                        psA[:, so:so + 512], ALU.mult, ALU.add),
                        deps=[tok_ymm[(lq, t, 15)], t_dv])
                    vb_deps = [tok_zl[idx], tvt]
                    if sc == 0 and idx >= 2:
                        vb_deps.append(tok_vw[idx - 2])
                    tok_vb[idx] = S.add("vector", lambda v, idx=idx, t=t, so=so: v.tensor_mul(
                        vb4[idx % 2][:, so:so + 512], vt4[:], sz4[t % 2][:, so:so + 512]),
                        deps=vb_deps)
                tok_vw[idx] = S.dma(
                    "gpsimd", v_d[t * 128:(t + 1) * 128, lq * 1024:(lq + 1) * 1024],
                    vb4[idx % 2][:], deps=[tok_vb[idx]])
        S.emit_phase("scan")
        p4.close()

        p5 = ExitStack()
        wo_bf = sbt("wo_bf", [128, 16 * D], BF16, p5)
        vch = sbt("vch", [128, 16 * 512], BF16, p5)
        stile = [sbt(f"stile{j}", [128, 512], BF16, p5) for j in range(2)]

        tko = []
        for kt in range(16):
            tko.append(S.dma("gpsimd", wo_bf[:, kt * D:(kt + 1) * D],
                             woutT[kt * 128:(kt + 1) * 128, :]))
        tok_vl = {}
        tok_se = {}
        tok_sw = {}
        tok_smm = {}
        for ch in range(8):
            for kt in range(16):
                deps = [tok_vw[63]]
                if ch > 0:
                    deps.append(tok_smm[(ch - 1, 7)])
                tok_vl[(ch, kt)] = S.dma(
                    "scalar", vch[:, kt * 512:(kt + 1) * 512],
                    v_d[kt * 128:(kt + 1) * 128, ch * 512:(ch + 1) * 512], deps=deps)
            for mt in range(8):
                gi = ch * 8 + mt
                war = [tok_se[divmod(gi - 2, 8)]] if gi >= 2 else []

                def sofn(t, ch=ch, mt=mt):
                    for kt in range(16):
                        ins = t.matmul(psA[:, (mt % 2) * 512:(mt % 2) * 512 + 512],
                                       wo_bf[:, kt * D + mt * 128:kt * D + mt * 128 + 128],
                                       vch[:, kt * 512:(kt + 1) * 512],
                                       start=(kt == 0), stop=(kt == 15))
                    return ins
                tok_smm[(ch, mt)] = S.add(
                    "tensor", sofn,
                    deps=[tko[15]] + [tok_vl[(ch, kt)] for kt in range(16)] + war)
                sdeps = [tok_smm[(ch, mt)]]
                if gi >= 2:
                    sdeps.append(tok_sw[divmod(gi - 2, 8)])
                tok_se[(ch, mt)] = S.add("vector", lambda v, mt=mt, gi=gi: v.tensor_copy(
                    stile[gi % 2][:], psA[:, (mt % 2) * 512:(mt % 2) * 512 + 512]),
                    deps=sdeps)
                tok_sw[(ch, mt)] = S.dma(
                    "gpsimd", SEND[mt * 128:(mt + 1) * 128, ch * 512:(ch + 1) * 512],
                    stile[gi % 2][:], deps=[tok_se[(ch, mt)]])
        S.emit_phase("wout")
        p5.close()
        pA_outer.close()

    ct = {"ib": t_ib, "if": t_if, "o1": t_o1, "o2": t_o2, "o3": t_o3}
    return ct, [tok_sw[(7, 7)], tok_gw[31]]



def emit_bridge(nc, S, ctx, sbt, psA, psB, io, cm, ct, send_done):
    SEND = io["SEND"]
    AGG = io["AGG"]
    sfL = io["sfL"]
    srL = io["srL"]
    g0L = io["g0L"]
    jrevh = io["jrevh"]
    identbf = cm["identbf"]
    t_ib = ct["ib"]

    tcoll = S.add("gpsimd", lambda g: g.collective_compute(
        "AllGather", mybir.AluOpType.bypass,
        replica_groups=[[0, 1], [2, 3], [4, 5], [6, 7]],
        ins=[SEND.opt()], outs=[AGG.opt()]),
        deps=send_done, inc="bare")

    brs = ExitStack()
    jrev_sb = sbt("jrev_sb", [128, 128], BF16, brs)
    t_jr = S.dma("sync", jrev_sb[:], jrevh[:, :])
    mnS = sbt("br_mnS", [128, 2048], BF16, brs)
    mnS2 = sbt("br_mnS2", [128, 2048], BF16, brs)
    a0S = sbt("br_a0S", [128, 2048], BF16, brs)
    a1S = sbt("br_a1S", [128, 2048], BF16, brs)
    mnG = sbt("br_mnG", [128, 2048], BF16, brs)
    a0G = sbt("br_a0G", [128, 2048], BF16, brs)
    a1G = sbt("br_a1G", [128, 2048], BF16, brs)
    fxS = sbt("br_fxS", [128, 2048], F32, brs)
    fxS2 = sbt("br_fxS2", [128, 2048], BF16, brs)
    fxG = sbt("br_fxG", [128, 2048], F32, brs)
    fxG2 = sbt("br_fxG2", [128, 2048], BF16, brs)
    tokP = sbt("br_tokP", [128, 1024], BF16, brs)
    stg = [sbt(f"br_stg{j}", [128, 1024], BF16, brs) for j in range(3)]
    brt = {}

    def slab(base_ap, r0, c0):
        return base_ap[r0:r0 + D, c0:c0 + 256].rearrange("(kt p) t -> p kt t", p=128)

    prev = {}
    tr_ev = {}
    tr_idx = 0
    for ip in range(16):
        t0 = 256 * ip
        rc0 = L - t0 - 256
        tmn = S.dma("sync", mnS[:], slab(SEND, 0, t0),
                    deps=[send_done[0], send_done[1], prev.get("mnS")])
        tmn2 = S.dma("sync", mnS2[:], slab(SEND, 0, rc0),
                     deps=[send_done[0], send_done[1], prev.get("mnS2")])
        ta0 = S.dma("scalar", a0S[:], slab(AGG, 0, rc0),
                    deps=[tcoll, prev.get("a0S")])
        ta1 = S.dma("gpsimd", a1S[:], slab(AGG, D, rc0),
                    deps=[tcoll, prev.get("a1S")])
        f1 = S.add("vector", lambda v: v.tensor_sub(fxS[:], a0S[:], mnS2[:]),
                   deps=[ta0, tmn2])
        prev["mnS2"] = f1
        f2 = S.add("vector", lambda v: v.tensor_add(fxS2[:], fxS[:], a1S[:]),
                   deps=[ta1])
        prev["a0S"] = f1
        prev["a1S"] = f2
        if ip < 8:
            gmine = True
            tmg = S.dma("sync", mnG[:], slab(SEND, 0, L + t0),
                        deps=[send_done[1], prev.get("mnG")])
            g2 = None
        else:
            gmine = False
            gj0 = L - t0 - 256
            tmg = S.dma("sync", mnG[:], slab(SEND, 0, L + gj0),
                        deps=[send_done[1], prev.get("mnG")])
            tg0 = S.dma("scalar", a0G[:], slab(AGG, 0, L + gj0),
                        deps=[tcoll, prev.get("a0G")])
            tg1 = S.dma("gpsimd", a1G[:], slab(AGG, D, L + gj0),
                        deps=[tcoll, prev.get("a1G")])
            g1 = S.add("vector", lambda v: v.tensor_sub(fxG[:], a0G[:], mnG[:]),
                       deps=[tg0, tmg])
            g2 = S.add("vector", lambda v: v.tensor_add(fxG2[:], fxG[:], a1G[:]),
                       deps=[tg1])
            prev["a0G"] = g1
            prev["a1G"] = g2
        last_mn_use = None
        last_fx_use = None
        last_mg_use = None
        last_fg_use = None
        for st in range(2):
            r0 = t0 + st * 128
            ev_mine = []
            for kt in range(8):
                slot = (tr_idx % 2) * 128
                war = [tr_ev.get(tr_idx - 2)]
                ttr = S.add("tensor", lambda t, kt=kt, st=st, slot=slot: t.transpose(
                    psB[:, slot:slot + 128],
                    mnS[:, kt * 256 + st * 128:kt * 256 + (st + 1) * 128],
                    identbf[:]), deps=[tmn, t_ib, prev.get("stg0")] + war)
                tev = S.add("vector", lambda v, kt=kt, slot=slot: v.tensor_copy(
                    stg[0][:, kt * 128:(kt + 1) * 128], psB[:, slot:slot + 128]),
                    deps=[ttr])
                tr_ev[tr_idx] = tev
                tr_idx += 1
                ev_mine.append(tev)
                last_mn_use = ttr
            ev_part = []
            for kt in range(8):
                slot = (tr_idx % 2) * 128
                war = [tr_ev.get(tr_idx - 2)]
                ttr = S.add("tensor", lambda t, kt=kt, st=st, slot=slot: t.transpose(
                    psB[:, slot:slot + 128],
                    fxS2[:, kt * 256 + (1 - st) * 128:kt * 256 + (2 - st) * 128],
                    identbf[:]), deps=[f2, t_ib, prev.get("tokP")] + war)
                tev = S.add("vector", lambda v, kt=kt, slot=slot: v.tensor_copy(
                    tokP[:, kt * 128:(kt + 1) * 128], psB[:, slot:slot + 128]),
                    deps=[ttr])
                tr_ev[tr_idx] = tev
                tr_idx += 1
                ev_part.append(tev)
                last_fx_use = ttr
            tj1 = S.add("tensor", lambda t: t.matmul(
                psA[:, 0:512], jrev_sb[:], tokP[:, 0:512], start=True, stop=True),
                deps=ev_part + [t_jr, prev.get("psJ1")])
            tj2 = S.add("tensor", lambda t: t.matmul(
                psA[:, 512:1024], jrev_sb[:], tokP[:, 512:1024], start=True, stop=True),
                deps=[tj1, prev.get("psJ2")])
            prev["tokP"] = tj2
            evj1 = S.add("vector", lambda v: v.tensor_copy(
                stg[1][:, 0:512], psA[:, 0:512]), deps=[tj1, prev.get("stg1")])
            evj = S.add("vector", lambda v: v.tensor_copy(
                stg[1][:, 512:1024], psA[:, 512:1024]), deps=[tj2])
            prev["psJ1"] = evj1
            prev["psJ2"] = evj
            ev_g = []
            if gmine:
                for kt in range(8):
                    slot = (tr_idx % 2) * 128
                    war = [tr_ev.get(tr_idx - 2)]
                    ttr = S.add("tensor", lambda t, kt=kt, st=st, slot=slot: t.transpose(
                        psB[:, slot:slot + 128],
                        mnG[:, kt * 256 + st * 128:kt * 256 + (st + 1) * 128],
                        identbf[:]), deps=[tmg, t_ib, prev.get("stg2")] + war)
                    tev = S.add("vector", lambda v, kt=kt, slot=slot: v.tensor_copy(
                        stg[2][:, kt * 128:(kt + 1) * 128], psB[:, slot:slot + 128]),
                        deps=[ttr])
                    tr_ev[tr_idx] = tev
                    tr_idx += 1
                    ev_g.append(tev)
                    last_mg_use = ttr
                wg_deps = ev_g
            else:
                for kt in range(8):
                    slot = (tr_idx % 2) * 128
                    war = [tr_ev.get(tr_idx - 2)]
                    ttr = S.add("tensor", lambda t, kt=kt, st=st, slot=slot: t.transpose(
                        psB[:, slot:slot + 128],
                        fxG2[:, kt * 256 + (1 - st) * 128:kt * 256 + (2 - st) * 128],
                        identbf[:]), deps=[g2, t_ib, prev.get("tokP")] + war)
                    tev = S.add("vector", lambda v, kt=kt, slot=slot: v.tensor_copy(
                        tokP[:, kt * 128:(kt + 1) * 128], psB[:, slot:slot + 128]),
                        deps=[ttr])
                    tr_ev[tr_idx] = tev
                    tr_idx += 1
                    ev_g.append(tev)
                    last_fg_use = ttr
                tjg1 = S.add("tensor", lambda t: t.matmul(
                    psA[:, 0:512], jrev_sb[:], tokP[:, 0:512], start=True, stop=True),
                    deps=ev_g + [t_jr, prev.get("psJ1")])
                tjg2 = S.add("tensor", lambda t: t.matmul(
                    psA[:, 512:1024], jrev_sb[:], tokP[:, 512:1024],
                    start=True, stop=True), deps=[tjg1, prev.get("psJ2")])
                prev["tokP"] = tjg2
                evjg1 = S.add("vector", lambda v: v.tensor_copy(
                    stg[2][:, 0:512], psA[:, 0:512]), deps=[tjg1, prev.get("stg2")])
                evjg = S.add("vector", lambda v: v.tensor_copy(
                    stg[2][:, 512:1024], psA[:, 512:1024]), deps=[tjg2])
                prev["psJ1"] = evjg1
                prev["psJ2"] = evjg
                wg_deps = [evjg]
            w0 = S.dma("gpsimd", sfL[r0:r0 + 128, :], stg[0][:], deps=ev_mine)
            w1 = S.dma("gpsimd", srL[r0:r0 + 128, :], stg[1][:], deps=[evj])
            w2 = S.dma("gpsimd", g0L[r0:r0 + 128, :], stg[2][:], deps=wg_deps)
            prev["stg0"] = w0
            prev["stg1"] = w1
            prev["stg2"] = w2
            brt[2 * ip + st] = w2
        prev["mnS"] = last_mn_use
        prev["mnG"] = last_mg_use if gmine else last_mg_use or tmg
        if not gmine:
            prev["mnG"] = g1
    S.emit_phase("bridge")
    brs.close()
    return brt



def emit_B(nc, S, ctx, sbt, psA, psB, io, cm, ct, brt):
    x = io["x"]
    sfL = io["sfL"]
    srL = io["srL"]
    g0L = io["g0L"]
    attw = io["attw"]
    routT = io["routT"]
    nw = io["nw"]
    guT = io["guT"]
    dnT = io["dnT"]
    alpha = io["alpha"]
    outT = io["outT"]
    comb_d = io["comb_d"]
    x1t_d = io["x1t_d"]
    proj_d = io["proj_d"]
    xn1_d = io["xn1_d"]
    rs_d = io["rs_d"]

    QC = 256
    NQC = LH // QC

    identbf = cm["identbf"]
    identf = cm["identf"]
    onesf_row = cm["onesf_row"]
    onesf_col = cm["onesf_col"]
    onesbf_col = cm["onesbf_col"]
    t_ib = ct["ib"]
    t_if = ct["if"]
    t_o1 = ct["o1"]
    t_o2 = ct["o2"]
    t_o3 = ct["o3"]

    if True:
        onesbf_row = sbt("onesbf_row", [1, 128], BF16)
        t_o4 = S.add("gpsimd", lambda g: g.memset(onesbf_row[:], 1.0))
        sscol = sbt("sscol", [128, 32])
        rstd2col = sbt("rstd2col", [128, 32])
        acol = sbt("acol", [128, 2])
        xn1T = sbt("xn1T", [128, 8 * L], BF16)
        b12 = ExitStack()
        xn1raw = sbt("xn1raw", [128, 32 * D], BF16, b12, side="right")
        rstd2row = sbt("rstd2row", [1, LH], F32)
        tok_cpb = {}

        b1 = ExitStack()
        xt = [sbt(f"xt{j}", [128, D], F32, b1) for j in range(2)]
        sfb = [sbt(f"sfb{j}", [128, D], BF16, b1) for j in range(2)]
        srb = [sbt(f"srb{j}", [128, D], BF16, b1) for j in range(2)]
        gtb = [sbt(f"gtb{j}", [128, D], BF16, b1) for j in range(2)]
        st_ = [sbt(f"st{j}", [128, D], F32, b1) for j in range(2)]
        x1t = sbt("x1t", [128, D], F32, b1)
        xsq = [sbt(f"xsq{j}", [128, D], F32, b1) for j in range(2)]
        stat = sbt("stat", [128, 32], F32, b1)
        x1pc = sbt("x1pc", [128, 128], F32, b1)

        tok_norm = [None] * 32
        tok_tr_last = [None] * 32
        tok_rsq = [None] * 32
        tok_x1w = {}
        for i in range(32):
            sb = (i % 2) * 16
            deps = [tok_norm[i - 2]] if i >= 2 else []
            bdeps = deps + [brt[i]]
            tlx = S.dma("sync", xt[i % 2][:], x[i * 128:(i + 1) * 128, :], deps=bdeps)
            tls = S.dma("sync", sfb[i % 2][:], sfL[i * 128:(i + 1) * 128, :], deps=bdeps)
            tlr = S.dma("scalar", srb[i % 2][:], srL[i * 128:(i + 1) * 128, :], deps=bdeps)
            tlg = S.dma("scalar", gtb[i % 2][:], g0L[i * 128:(i + 1) * 128, :], deps=bdeps)
            ta = S.add("vector", lambda v, i=i: v.tensor_add(
                st_[i % 2][:], sfb[i % 2][:], srb[i % 2][:]), deps=[tls, tlr])
            tb = S.add("vector", lambda v, i=i: v.tensor_mul(
                st_[i % 2][:], st_[i % 2][:], gtb[i % 2][:]), deps=[tlg])
            x1deps = [tlx]
            if i >= 1:
                x1deps.append(tok_norm[i - 1])
                if (i - 1) < 16:
                    x1deps.append(tok_x1w[(i - 1, 7)])
            tx1 = S.add("vector", lambda v, i=i: v.tensor_add(
                x1t[:], xt[i % 2][:], st_[i % 2][:]), deps=x1deps)
            tsq = S.add("scalar", lambda s, i=i: s.activation(xsq[i % 2][:], x1t[:],
                                                              AF.Square),
                        deps=[tx1] + ([tok_rsq[i - 2]] if i >= 2 else []))
            S.add("vector", lambda v, sb=sb: v.reduce_sum(
                stat[:, sb + 0:sb + 1], x1t[:], axis=AX.X), deps=[tx1])
            tok_rsq[i] = S.add("vector", lambda v, i=i, sb=sb: v.reduce_sum(
                stat[:, sb + 1:sb + 2], xsq[i % 2][:], axis=AX.X), deps=[tsq])
            S.add("vector", lambda v, sb=sb: v.tensor_scalar_mul(
                stat[:, sb + 2:sb + 3], stat[:, sb + 0:sb + 1], 1.0 / D))
            S.add("vector", lambda v, sb=sb: v.tensor_scalar_mul(
                stat[:, sb + 3:sb + 4], stat[:, sb + 1:sb + 2], 1.0 / D))
            S.add("vector", lambda v, sb=sb: v.tensor_mul(
                stat[:, sb + 4:sb + 5], stat[:, sb + 2:sb + 3], stat[:, sb + 2:sb + 3]))
            S.add("vector", lambda v, sb=sb: v.tensor_sub(
                stat[:, sb + 5:sb + 6], stat[:, sb + 3:sb + 4], stat[:, sb + 4:sb + 5]))
            tvp = S.add("vector", lambda v, sb=sb: v.tensor_scalar_add(
                stat[:, sb + 6:sb + 7], stat[:, sb + 5:sb + 6], 1e-5))
            tsqr = S.add("scalar", lambda s, sb=sb: s.activation(
                stat[:, sb + 7:sb + 8], stat[:, sb + 6:sb + 7], AF.Sqrt), deps=[tvp])
            trs = S.add("vector", lambda v, sb=sb: v.reciprocal(
                stat[:, sb + 8:sb + 9], stat[:, sb + 7:sb + 8]), deps=[tsqr])
            tnb = S.add("vector", lambda v, sb=sb: v.tensor_scalar(
                stat[:, sb + 9:sb + 10], stat[:, sb + 2:sb + 3], stat[:, sb + 8:sb + 9],
                -1.0, ALU.mult, ALU.mult))
            ndeps = [tnb]
            if i >= 1:
                ndeps.append(tok_tr_last[i - 1])
            tok_norm[i] = S.add("scalar", lambda s, i=i, sb=sb: s.activation(
                xn1raw[:, i * D:(i + 1) * D], x1t[:], AF.Identity,
                bias=stat[:, sb + 9:sb + 10], scale=stat[:, sb + 8:sb + 9]), deps=ndeps)
            tsq2 = S.add("scalar", lambda s, i=i: s.activation(
                xsq[i % 2][:], xn1raw[:, i * D:(i + 1) * D], AF.Square),
                deps=[tok_norm[i]])
            S.add("vector", lambda v, i=i: v.reduce_sum(
                sscol[:, i:i + 1], xsq[i % 2][:], axis=AX.X), deps=[tsq2])
            S.add("vector", lambda v, i=i, sb=sb: v.tensor_scalar(
                stat[:, sb + 10:sb + 11], sscol[:, i:i + 1], 1.0 / D, 1e-6,
                ALU.mult, ALU.add))
            tsq3 = S.add("scalar", lambda s, sb=sb: s.activation(
                stat[:, sb + 11:sb + 12], stat[:, sb + 10:sb + 11], AF.Sqrt))
            S.add("vector", lambda v, i=i, sb=sb: v.reciprocal(
                rstd2col[:, i:i + 1], stat[:, sb + 11:sb + 12]), deps=[tsq3])
            for k in range(8):
                war = [tok_cpb[i * 8 + k - 2]] if (i * 8 + k) >= 2 else []
                ttr = S.add("tensor", lambda t, i=i, k=k: t.transpose(
                    psB[:, (k % 2) * 1024:(k % 2) * 1024 + 128],
                    xn1raw[:, i * D + k * 128:i * D + (k + 1) * 128], identbf[:]),
                    deps=[tok_norm[i], t_ib] + war)
                tok_cpb[i * 8 + k] = S.add("vector", lambda v, i=i, k=k: v.tensor_copy(
                    xn1T[:, k * L + i * 128:k * L + (i + 1) * 128],
                    psB[:, (k % 2) * 1024:(k % 2) * 1024 + 128]), deps=[ttr])
                tok_tr_last[i] = ttr
            if i < 16:
                for k in range(8):
                    slot = (k % 2) * 512
                    war = [tok_x1w[(i, k - 2)]] if k >= 2 else (
                        [tok_x1w[(i - 1, 6)], tok_x1w[(i - 1, 7)]] if i >= 1 else [])
                    ttf = S.add("tensor", lambda t, i=i, k=k, slot=slot: t.transpose(
                        psA[:, slot:slot + 128], x1t[:, k * 128:(k + 1) * 128],
                        identf[:]), deps=[tx1, t_if] + war)
                    cf_deps = [ttf]
                    if k >= 1:
                        cf_deps.append(tok_x1w[(i, k - 1)])
                    elif i >= 1:
                        cf_deps.append(tok_x1w[(i - 1, 7)])
                    tcf = S.add("vector", lambda v, slot=slot: v.tensor_copy(
                        x1pc[:], psA[:, slot:slot + 128]), deps=cf_deps)
                    tok_x1w[(i, k)] = S.dma(
                        "gpsimd", x1t_d[k * 128:(k + 1) * 128, i * 128:(i + 1) * 128],
                        x1pc[:], deps=[tcf])
        trt2 = S.add("tensor", lambda t: t.transpose(
            psA[0:32, 1024:1152], rstd2col[:, 0:32], identf[:]),
            deps=[S.last_comp["vector"], t_if])
        trc2 = S.add("vector", lambda v: v.tensor_copy(x1pc[0:32, 0:128],
                                                       psA[0:32, 1024:1152]),
                     deps=[trt2, tok_x1w[(15, 7)]])
        trq1 = S.dma("gpsimd", rs_d[:, :], x1pc[0:32, 0:128], deps=[trc2])
        trq2 = S.dma("gpsimd", rstd2row[0:1, :],
                     rs_d[0:16, :].rearrange("t p -> (t p)").unsqueeze(0), deps=[trq1])
        for kq in range(8):
            S.dma("gpsimd", xn1_d[:, kq * L:(kq + 1) * L],
                  xn1T[:, kq * L:(kq + 1) * L], deps=[tok_cpb[255]])
        tok_xnd = S.ops["gpsimd"][-1]
        tok_xnd = ("gpsimd", tok_xnd)
        S.emit_phase("b1")
        b1.close()

        b2 = ExitStack()
        attw_bf = sbt("attw_bf", [128, 8 * D], BF16, b2)
        ebuf = sbt("ebuf", [128, 32 * QC], BF16, b2)
        obuf = sbt("obuf", [128, 8 * QC], BF16, b2)
        sqh = sbt("sqh", [64, 2048], BF16, b2)
        ssq_row = sbt("ssq_row", [1, L], BF16, b2)
        m_row = sbt("m_row", [1, L], BF16, b2)
        kmax = sbt("kmax", [1, 2], F32, b2)
        mbp = sbt("mbp", [128, QC], F32, b2)
        zrow = sbt("zrow", [1, QC], F32, b2)
        zrb = sbt("zrb", [128, QC], F32, b2)
        sc_f = sbt("sc_f", [128, QC], F32, b2)
        prj = sbt("prj", [128, QC], F32, b2)

        tok_pw = {}
        prev_exp = {}
        prev_ev = {}
        prev_olast = None
        prev_attw = None
        for h in range(16):
            pt, r0 = h // 2, (h % 2) * 64
            ta_ld = []
            for ptw in range(8):
                adeps = [prev_attw] if prev_attw else []
                ta_ld.append(S.dma(
                    "gpsimd", attw_bf[:, ptw * D:(ptw + 1) * D],
                    attw[h * D + ptw * 128:h * D + (ptw + 1) * 128, :], deps=adeps))
            tsq_l = []
            for hf in range(2):
                sq_deps = [tok_cpb[255]]
                tsq_l.append(S.add("vector", lambda v, pt=pt, r0=r0, hf=hf: v.tensor_mul(
                    sqh[:, 0:2048],
                    xn1T[r0:r0 + 64, pt * L + hf * 2048:pt * L + (hf + 1) * 2048],
                    xn1T[r0:r0 + 64, pt * L + hf * 2048:pt * L + (hf + 1) * 2048]),
                    deps=sq_deps))
            tzp = None
            for chq in range(8):
                tmm = S.add("tensor", lambda t, chq=chq: t.matmul(
                    psA[0:1, 1536:2048], onesbf_col[0:64, 0:1],
                    sqh[:, (chq % 4) * 512:(chq % 4) * 512 + 512],
                    start=True, stop=True),
                    deps=[tsq_l[chq // 4], t_o3] + ([tzp] if tzp else []))
                tzp = S.add("vector", lambda v, chq=chq: v.tensor_copy(
                    ssq_row[0:1, chq * 512:(chq + 1) * 512], psA[0:1, 1536:2048]),
                    deps=[tmm])
            tqn = S.add("scalar", lambda s: s.activation(m_row[0:1, :], ssq_row[0:1, :],
                                                         AF.Sqrt), deps=[tzp])
            tmx = S.add("vector", lambda v: v.reduce_max(kmax[0:1, 0:1], m_row[0:1, :],
                                                         axis=AX.X), deps=[tqn])
            tsc = S.add("vector", lambda v: v.tensor_scalar(
                kmax[0:1, 1:2], kmax[0:1, 0:1], 0.1275, 1.0, ALU.mult, ALU.add))
            tmr = S.add("vector", lambda v: v.tensor_single_scalar(
                m_row[0:1, :], m_row[0:1, :], kmax[0:1, 1:2], ALU.mult), deps=[tsc])
            for qc in range(NQC):
                tmb = S.add("tensor", lambda t, qc=qc: t.matmul(
                    psA[:, 1536:1536 + QC], onesbf_row[0:1, :],
                    m_row[0:1, qc * QC:(qc + 1) * QC], start=True, stop=True),
                    deps=[tmr, t_o4] + ([tok_pw[(h, qc - 1, 7)]] if qc >= 1 else
                                        ([tok_pw[(h - 1, NQC - 1, 7)]] if h >= 1 else [])))
                tmbe = S.add("vector", lambda v: v.tensor_copy(mbp[:], psA[:, 1536:1536 + QC]),
                             deps=[tmb])
                tok_exp = {}
                for stt_ in range(32):
                    slot = (stt_ % 2) * 512
                    war = [prev_ev.get(stt_ % 2)]
                    if stt_ == 0:
                        war.append(prev_olast)
                    tsmm = S.add("tensor", lambda t, pt=pt, r0=r0, stt_=stt_, qc=qc, slot=slot:
                                 t.matmul(psA[:, slot:slot + QC],
                                          xn1T[r0:r0 + 64, pt * L + stt_ * 128:pt * L + (stt_ + 1) * 128],
                                          xn1T[r0:r0 + 64, pt * L + qc * QC:pt * L + (qc + 1) * QC],
                                          start=True, stop=True),
                                 deps=[tok_cpb[255]] + war)
                    tev = S.add("vector", lambda v, slot=slot: v.scalar_tensor_tensor(
                        sc_f[:], psA[:, slot:slot + QC], 0.125, mbp[:],
                        ALU.mult, ALU.subtract),
                        deps=[tsmm, tmbe] + ([prev_exp[0]] if 0 in prev_exp else []))
                    tok_exp[stt_] = S.add("scalar", lambda s, stt_=stt_: s.activation(
                        ebuf[:, stt_ * QC:(stt_ + 1) * QC], sc_f[:], AF.Exp),
                        deps=[tev])
                    prev_exp[0] = tok_exp[stt_]
                    prev_ev[stt_ % 2] = tev
                tz = None
                for stt_ in range(32):
                    tz = S.add("tensor", lambda t, stt_=stt_: t.matmul(
                        psA[0:1, 1536:1536 + QC], onesbf_col[:, 0:1],
                        ebuf[:, stt_ * QC:(stt_ + 1) * QC],
                        start=(stt_ == 0), stop=(stt_ == 31)),
                        deps=[tok_exp[stt_], t_o3] + ([tmbe] if stt_ == 0 else []))
                tze = S.add("vector", lambda v: v.tensor_copy(zrow[0:1, :],
                                                              psA[0:1, 1536:1536 + QC]),
                            deps=[tz])
                trz = S.add("vector", lambda v: v.reciprocal(zrow[0:1, :], zrow[0:1, :]))
                tzb = S.add("tensor", lambda t: t.matmul(
                    psA[:, 2048:2048 + QC], onesf_row[0:1, :], zrow[0:1, :],
                    start=True, stop=True), deps=[trz, t_o1])
                tzbe = S.add("vector", lambda v: v.tensor_copy(zrb[:], psA[:, 2048:2048 + QC]),
                             deps=[tzb])
                tprev_o = None
                for cb in range(8):
                    to = None
                    for stt_ in range(32):
                        to = S.add("tensor", lambda t, cb=cb, stt_=stt_: t.matmul(
                            psA[:, 1024:1024 + QC],
                            xn1raw[:, stt_ * D + cb * 128:stt_ * D + cb * 128 + 128],
                            ebuf[:, stt_ * QC:(stt_ + 1) * QC],
                            start=(stt_ == 0), stop=(stt_ == 31)),
                            deps=[tok_exp[31]] + ([tprev_o] if (stt_ == 0 and tprev_o) else []))
                    tprev_o = S.add("vector", lambda v, cb=cb: v.tensor_mul(
                        obuf[:, cb * QC:(cb + 1) * QC], psA[:, 1024:1024 + QC], zrb[:]),
                        deps=[to, tzbe])
                    prev_olast = to
                for mt in range(8):
                    pslot = 2560 if mt % 2 == 0 else 1536
                    tp = None
                    for cb in range(8):
                        tp = S.add("tensor", lambda t, mt=mt, cb=cb, pslot=pslot: t.matmul(
                            psA[:, pslot:pslot + QC],
                            attw_bf[:, cb * D + mt * 128:cb * D + mt * 128 + 128],
                            obuf[:, cb * QC:(cb + 1) * QC],
                            start=(cb == 0), stop=(cb == 7)),
                            deps=[tprev_o, ta_ld[cb]] +
                                 ([tok_pw[(h, qc, mt - 2)]] if mt >= 2 else
                                  ([tok_pw[(h, qc - 1, mt + 6)]] if qc >= 1 else
                                   ([tok_pw[(h - 1, NQC - 1, mt + 6)]] if (h >= 1 and mt >= 0 and mt < 2) else []))))
                    prj_war = []
                    if mt >= 1:
                        prj_war.append(tok_pw[(h, qc, mt - 1)])
                    elif qc >= 1:
                        prj_war.append(tok_pw[(h, qc - 1, 7)])
                    elif h >= 1:
                        prj_war.append(tok_pw[(h - 1, NQC - 1, 7)])
                    if h == 0:
                        tv = S.add(
                            "vector", lambda v, pslot=pslot: v.tensor_copy(
                                prj[:], psA[:, pslot:pslot + QC]), deps=[tp] + prj_war)
                    else:
                        tld = S.dma("sync", prj[:],
                                    proj_d[mt * 128:(mt + 1) * 128, qc * QC:(qc + 1) * QC],
                                    deps=[tok_pw[(h - 1, qc, mt)]] + prj_war)
                        tv = S.add(
                            "vector", lambda v, pslot=pslot: v.tensor_add(
                                prj[:], prj[:], psA[:, pslot:pslot + QC]),
                            deps=[tp, tld])
                    tok_pw[(h, qc, mt)] = S.dma(
                        "scalar", proj_d[mt * 128:(mt + 1) * 128, qc * QC:(qc + 1) * QC],
                        prj[:], deps=[tv])
            prev_attw = ta_ld[7]
        S.emit_phase("b2")
        b2.close()
        b12.close()

        b3 = ExitStack()
        moeacc = sbt("moeacc", [128, 8 * LH], F32, b3, side="right")
        nwcol = sbt("nwcol", [128, 8], F32, b3)
        combrow = sbt("combrow", [8, L], BF16, b3)
        b3a = ExitStack()
        rw_bf = sbt("rw_bf", [128, 8 * NEXP], BF16, b3a)
        logT = sbt("logT", [8, L], F32, b3a)
        ltile = sbt("ltile", [128, 8], F32, b3a)
        top8 = sbt("top8", [128, 8], F32, b3a)
        l2t = sbt("l2t", [128, 8], F32, b3a)
        l3t = sbt("l3t", [128, 8], F32, b3a)
        msk = sbt("msk", [128, 8], F32, b3a)
        wcol = sbt("wcol", [128, 4], F32, b3a)
        combt = sbt("combt", [128, 8], F32, b3a)
        mrv = sbt("mrv", [128, 8], F32, b3a)


        tnw = S.dma("sync", nwcol[:, :], nw.rearrange("(t p) o -> p t o", p=128),
                    slow=True)
        trt = S.dma("gpsimd", rw_bf[:, :], routT.rearrange("(t p) e -> p t e", p=128))
        tfold = []
        for kt in range(8):
            tfold.append(S.add("vector", lambda v, kt=kt: v.tensor_single_scalar(
                rw_bf[:, kt * NEXP:(kt + 1) * NEXP], rw_bf[:, kt * NEXP:(kt + 1) * NEXP],
                nwcol[:, kt:kt + 1], ALU.mult), deps=[tnw, trt]))
        tle = None
        for chl in range(8):
            def lmfn(t, chl=chl):
                for kt in range(8):
                    ins = t.matmul(psA[0:8, 0:512],
                                   rw_bf[:, kt * NEXP:(kt + 1) * NEXP],
                                   xn1T[:, kt * L + chl * 512:kt * L + (chl + 1) * 512],
                                   start=(kt == 0), stop=(kt == 7))
                return ins
            tlm = S.add("tensor", lmfn, deps=tfold + ([tle] if tle else []))
            tle = S.add("vector", lambda v, chl=chl: v.tensor_copy(
                logT[:, chl * 512:(chl + 1) * 512], psA[0:8, 0:512]), deps=[tlm])
        tok_cr = {}
        for i in range(32):
            ttr = S.add("tensor", lambda t, i=i: t.transpose(
                psA[0:128, 512:520], logT[0:8, i * 128:(i + 1) * 128],
                identf[0:8, 0:8]),
                deps=[tle, t_if] + ([tok_cr[i - 1]] if i >= 1 else []))
            tlt = S.add("vector", lambda v, i=i: v.tensor_single_scalar(
                ltile[:], psA[:, 512:640][:, 0:8], rstd2col[:, i:i + 1], ALU.mult),
                deps=[ttr])
            t8 = S.add("vector", lambda v: v.max(top8[:, 0:8], ltile[:]))
            if i == 0:
                tok_mrv = S.add("gpsimd", lambda g: g.memset(mrv[:], 1e30), deps=[tle])
            S.add("vector", lambda v: v.tensor_copy(mrv[:, 0:1], top8[:, 0:1]),
                  deps=[tok_mrv] if i == 0 else [])
            tm1 = S.add("vector", lambda v: v.match_replace(
                l2t[:], mrv[:, 0:8], ltile[:], -1e30))
            S.add("vector", lambda v: v.tensor_copy(mrv[:, 0:1], top8[:, 1:2]))
            tm2 = S.add("vector", lambda v: v.match_replace(
                l3t[:], mrv[:, 0:8], l2t[:], -1e30))
            td = S.add("vector", lambda v: v.tensor_sub(
                wcol[:, 0:1], top8[:, 0:1], top8[:, 1:2]))
            tsg = S.add("scalar", lambda s: s.activation(wcol[:, 1:2], wcol[:, 0:1],
                                                         AF.Sigmoid), deps=[td])
            tw2 = S.add("vector", lambda v: v.tensor_scalar(
                wcol[:, 2:3], wcol[:, 1:2], -1.0, 1.0, ALU.mult, ALU.add), deps=[tsg])
            tk1 = S.add("vector", lambda v: v.tensor_tensor(
                msk[:], ltile[:], l2t[:], ALU.not_equal))
            tc1 = S.add("vector", lambda v: v.tensor_single_scalar(
                combt[:], msk[:], wcol[:, 1:2], ALU.mult))
            tk2 = S.add("vector", lambda v: v.tensor_tensor(
                msk[:], l2t[:], l3t[:], ALU.not_equal))
            tc2 = S.add("vector", lambda v: v.scalar_tensor_tensor(
                combt[:], msk[:], wcol[:, 2:3], combt[:], ALU.mult, ALU.add))
            tc3 = S.add("vector", lambda v, i=i: v.tensor_single_scalar(
                combt[:], combt[:], rstd2col[:, i:i + 1], ALU.mult))
            ttb = S.add("tensor", lambda t, i=i: t.transpose(
                psA[0:8, 512:640], combt[:, 0:8], identf[:]), deps=[tc3, t_if])
            tok_cr[i] = S.add("vector", lambda v, i=i: v.tensor_copy(
                combrow[0:8, i * 128:(i + 1) * 128], psA[0:8, 512:640]), deps=[ttb])
        tcw = S.dma("sync", comb_d[:, :], combrow[:, :], deps=[tok_cr[31]])
        b3a.close()
        tal = S.dma("sync", acol[0:1, 0:1], alpha[0:1, 0:1])
        tac1 = S.add("vector", lambda v: v.tensor_scalar_min(acol[0:1, 0:1], acol[0:1, 0:1], 2.0),
                     deps=[tal])
        tac2 = S.add("vector", lambda v: v.tensor_scalar_max(acol[0:1, 0:1], acol[0:1, 0:1], -0.5))
        tab = S.add("tensor", lambda t: t.matmul(psA[:, 640:641], onesf_row[0:1, :],
                                                 acol[0:1, 0:1], start=True, stop=True),
                    deps=[tac2, t_o1])
        tabe = S.add("vector", lambda v: v.tensor_copy(acol[:, 1:2], psA[:, 640:641]),
                     deps=[tab])

        gslice = [sbt(f"gsl{j}", [128, 8 * 128], BF16, b3) for j in range(2)]
        dslice = [sbt(f"dsl{j}", [128, 22 * 128], BF16, b3) for j in range(2)]
        hidb = sbt("hidb", [128, 22 * 512], BF16, b3)
        gsil = sbt("gsil", [128, 512], BF16, b3)
        tmpf = sbt("tmpf", [128, 512], F32, b3)
        cbb = sbt("cbb", [128, 512], BF16, b3)
        rsb4 = [sbt(f"rsb{j}", [128, 512], F32, b3) for j in range(4)]
        tok_fold = None
        for ptf in range(8):
            tok_fold = S.add("vector", lambda v, ptf=ptf: v.tensor_single_scalar(
                xn1T[:, ptf * L:(ptf + 1) * L], xn1T[:, ptf * L:(ptf + 1) * L],
                nwcol[:, ptf:ptf + 1], ALU.mult), deps=[tnw, tok_xnd, tle])
        tok_ma = {}
        prev_gl = {}
        prev_dl = {}
        tok_rsbe = {}
        for e in range(NEXP):
            for qc in range(4):
                war_ma = ([tok_ma[(e, qc - 1, 7)]] if qc >= 1 else
                          ([tok_ma[(e - 1, 3, 7)]] if e >= 1 else []))
                tcbe = S.dma("sync", cbb[:, :],
                             comb_d[e:e + 1, qc * 512:(qc + 1) * 512].partition_broadcast(128),
                             deps=[tcw] + war_ma)
                if e == 0:
                    trsb = S.add("tensor", lambda t, qc=qc: t.matmul(
                        psA[:, 2048:2560], onesf_row[0:1, :],
                        rstd2row[0:1, qc * 512:(qc + 1) * 512], start=True, stop=True),
                        deps=[trq2, t_o1] + war_ma +
                             ([tok_rsbe[qc - 1]] if qc >= 1 else []))
                    tok_rsbe[qc] = S.add("vector", lambda v: v.tensor_copy(
                        rsb4[qc][:], psA[:, 2048:2560]), deps=[trsb])
                for mh in range(22):
                    for half, mtt in ((0, mh), (1, 22 + mh)):
                        gdeps = []
                        key = (qc * 44 + mtt) % 2
                        if (e, mtt) != (0, 0) or qc > 0:
                            pass
                        war = [prev_gl[key]] if key in prev_gl else [tcw, tle]
                        tgl = S.dma("gpsimd", gslice[key][:, :],
                                    guT[e, :, mtt * 128:(mtt + 1) * 128].rearrange(
                                        "(kt p) m -> p kt m", p=128), deps=war)
                        def gufn(t, key=key, qc=qc):
                            for kt in range(8):
                                ins = t.matmul(
                                    psA[:, (key) * 512:(key) * 512 + 512],
                                    gslice[key][:, kt * 128:(kt + 1) * 128],
                                    xn1T[:, kt * L + qc * 512:kt * L + (qc + 1) * 512],
                                    start=(kt == 0), stop=(kt == 7))
                            return ins
                        tgm = S.add("tensor", gufn, deps=[tgl, tok_fold])
                        prev_gl[key] = tgm
                        if half == 0:
                            tsc2 = S.add("vector", lambda v, key=key, qc=qc: v.tensor_mul(
                                tmpf[:], psA[:, key * 512:key * 512 + 512], rsb4[qc][:]),
                                deps=[tgm, tok_rsbe[qc]])
                            tgsil = S.add("scalar", lambda s: s.activation(
                                gsil[:], tmpf[:], AF.Silu), deps=[tsc2])
                        else:
                            tup = S.add("vector", lambda v, key=key: v.tensor_mul(
                                tmpf[:], psA[:, key * 512:key * 512 + 512], cbb[:]),
                                deps=[tgm, tcbe])
                            thid = S.add("vector", lambda v, mh=mh: v.tensor_mul(
                                hidb[:, mh * 512:(mh + 1) * 512], tmpf[:], gsil[:]),
                                deps=[tup, tgsil])
                for mtd in range(8):
                    key = mtd % 2
                    war = [prev_dl[key]] if key in prev_dl else [tcw, tle]
                    tdl = S.dma("gpsimd", dslice[key][:, :],
                                dnT[e, :, mtd * 128:(mtd + 1) * 128].rearrange(
                                    "(kt p) m -> p kt m", p=128), deps=war)
                    def dnfn(t, key=key, mtd=mtd):
                        for kt in range(22):
                            ins = t.matmul(
                                psA[:, 1024 + key * 512:1024 + key * 512 + 512],
                                dslice[key][:, kt * 128:(kt + 1) * 128],
                                hidb[:, kt * 512:(kt + 1) * 512],
                                start=(kt == 0), stop=(kt == 21))
                        return ins
                    tdm = S.add("tensor", dnfn,
                                deps=[tdl, thid] + ([tok_ma[(e, qc, mtd - 2)]]
                                                    if mtd >= 2 else
                                                    ([tok_ma[(e, qc - 1, mtd + 6)]] if qc >= 1 else
                                                     ([tok_ma[(e - 1, 3, mtd + 6)]] if e >= 1 else []))))
                    prev_dl[key] = tdm
                    if e == 0:
                        tok_ma[(e, qc, mtd)] = S.add(
                            "vector", lambda v, qc=qc, mtd=mtd, key=key: v.tensor_copy(
                                moeacc[:, mtd * LH + qc * 512:mtd * LH + (qc + 1) * 512],
                                psA[:, 1024 + key * 512:1024 + key * 512 + 512]),
                            deps=[tdm])
                    else:
                        tok_ma[(e, qc, mtd)] = S.add(
                            "vector", lambda v, qc=qc, mtd=mtd, key=key: v.tensor_add(
                                moeacc[:, mtd * LH + qc * 512:mtd * LH + (qc + 1) * 512],
                                moeacc[:, mtd * LH + qc * 512:mtd * LH + (qc + 1) * 512],
                                psA[:, 1024 + key * 512:1024 + key * 512 + 512]),
                            deps=[tdm])
        S.emit_phase("b3")
        b3.close()

        b4 = ExitStack()
        x1l = sbt("x1l", [128, 2048], F32, b4)
        s1l = sbt("s1l", [128, 2048], F32, b4)
        xnl = sbt("xnl", [128, 2048], BF16, b4)
        g1l = sbt("g1l", [128, 2048], BF16, b4)
        otl = sbt("otl", [128, 2048], F32, b4)
        tprev = None
        for ptv in range(8):
            dl = [tprev] if tprev else []
            tl1 = S.dma("sync", x1l[:], x1t_d[ptv * 128:(ptv + 1) * 128, :],
                        deps=[tok_x1w[(15, 7)]] + dl)
            tl2 = S.dma("scalar", s1l[:], proj_d[ptv * 128:(ptv + 1) * 128, :],
                        deps=[tok_pw[(15, NQC - 1, 7)]] + dl)
            tl3 = S.dma("sync", xnl[:], xn1_d[:, ptv * L:ptv * L + 2048],
                        deps=[tok_xnd] + dl)
            tg1 = S.add("vector", lambda v, ptv=ptv: v.scalar_tensor_tensor(
                otl[:], moeacc[:, ptv * LH:(ptv + 1) * LH], acol[:, 1:2],
                xnl[:], ALU.mult, ALU.add),
                deps=[tok_ma[(7, 3, ptv)], tabe, tl3])
            tg2 = S.add("scalar", lambda s: s.activation(g1l[:], otl[:], AF.Silu),
                        deps=[tg1])
            tg3 = S.add("vector", lambda v: v.tensor_mul(otl[:], s1l[:], g1l[:]),
                        deps=[tg2, tl2])
            tg4 = S.add("vector", lambda v: v.tensor_add(otl[:], otl[:], x1l[:]),
                        deps=[tl1])
            tprev = S.dma("gpsimd", outT[ptv * 128:(ptv + 1) * 128, :], otl[:],
                          deps=[tg4])
        S.add("gpsimd", lambda g: g.nop(), deps=[tprev])
        S.emit_phase("b4")
        b4.close()



def build_fused(paranoid=False, debug=False):
    nc = bass.Bass("TRN2", target_bir_lowering=False)
    io = {}
    DK = "ExternalOutput" if debug else "Internal"

    def din(name, shape, dt=F32):
        io[name] = nc.dram_tensor(name, shape, dt, kind="ExternalInput").ap()

    def dint(name, shape, dt=F32, dk="Internal"):
        io[name] = nc.dram_tensor(name, shape, dt, kind=dk).ap()

    din("x", [L, D])
    din("winT", [D, 2 * DI])
    din("wxT", [DI, 96])
    din("wdtT", [RK, DI])
    din("convw", [DI, 4])
    din("convb", [DI, 1])
    din("alog", [DI, NST])
    din("bdt", [DI, 1])
    din("dvec", [DI, 1])
    din("woutT", [DI, D])
    din("wupT", [D, DFF])
    din("wdnT", [DFF, D])
    din("idbf", [128, 128], BF16)
    din("idf32", [128, 128], F32)
    din("jrevh", [128, 128], BF16)
    din("attw", [NHEAD * D, D])
    din("routT", [D, NEXP])
    din("nw", [D, 1])
    din("guT", [NEXP, D, 2 * HID])
    din("dnT", [NEXP, HID, D])
    din("alpha", [1, 1])
    io["outT"] = nc.dram_tensor("outT", [D, LH], F32, kind="ExternalOutput").ap()
    dint("xssm_d", [DI, L], BF16)
    dint("sz_d", [DI, L], BF16)
    dint("v_d", [DI, L], BF16)
    dint("bc_d", [2 * NST, L], F32)
    dint("sq_d", [32, 128], F32)
    dint("h_d", [DFF, LH], BF16)
    dint("SEND", [D, L + LH], BF16)
    dint("AGG", [2 * D, L + LH], BF16)
    dint("sfL", [L, D], BF16, DK)
    dint("srL", [L, D], BF16, DK)
    dint("g0L", [L, D], BF16, DK)
    dint("comb_d", [NEXP, L], BF16)
    dint("x1t_d", [D, LH], F32, DK)
    dint("proj_d", [D, LH], F32, DK)
    dint("xn1_d", [128, 8 * L], BF16, DK)
    dint("rs_d", [32, 128], F32)

    ctx = ExitStack()
    with ctx:
        def sbt(name, shape, dt=F32, st=None, side=None):
            if side:
                return (st or ctx).enter_context(nc.sbuf_tensor(name, shape, dt, side=side))
            return (st or ctx).enter_context(nc.sbuf_tensor(name, shape, dt))

        S = Sched(nc, ctx, paranoid=paranoid)
        psA = ctx.enter_context(nc.psum_tensor("psA", [128, 3072], F32))
        psB = ctx.enter_context(nc.psum_tensor("psB", [128, 2048], BF16))

        cm = {
            "identbf": sbt("identbf", [128, 128], BF16),
            "identf": sbt("identf", [128, 128], F32),
            "onesf_row": sbt("onesf_row", [1, 128]),
            "onesf_col": sbt("onesf_col", [128, 1]),
            "onesbf_col": sbt("onesbf_col", [128, 1], BF16),
        }
        def sbtB(name, shape, dt=F32, st=None, side=None):
            return sbt("B" + name, shape, dt, st, side)

        ct, send_done = emit_A(nc, S, ctx, sbt, psA, psB, io, cm)
        brt = emit_bridge(nc, S, ctx, sbt, psA, psB, io, cm, ct, send_done)
        emit_B(nc, S, ctx, sbtB, psA, psB, io, cm, ct, brt)
    return nc



_CACHE = {}
PARANOID = True


def _prep_A_inputs(inp):
    c = np.ascontiguousarray
    out = {
        "winT": c(np.asarray(inp["L0_attn_W_in"]).T),
        "woutT": c(np.asarray(inp["L0_attn_W_out"]).T),
        "wupT": c(np.asarray(inp["L0_ffn_w_up"]).T),
        "wdnT": c(np.asarray(inp["L0_ffn_w_down"]).T),
        "idbf": np.eye(128, dtype=BF),
        "idf32": np.eye(128, dtype=np.float32),
    }
    return out


def _prep_A_dir(inp, d):
    c = np.ascontiguousarray
    sfx = "fwd" if d == 0 else "rev"
    return {
        "wxT": c(np.asarray(inp[f"L0_attn_Wx_{sfx}"]).T),
        "wdtT": c(np.asarray(inp[f"L0_attn_Wdt_{sfx}"]).T),
        "convw": c(np.asarray(inp[f"L0_attn_conv_w_{sfx}"])),
        "convb": c(np.asarray(inp[f"L0_attn_conv_b_{sfx}"]).reshape(DI, 1)),
        "alog": c(np.asarray(inp[f"L0_attn_A_log_{sfx}"])),
        "bdt": c(np.asarray(inp[f"L0_attn_bdt_{sfx}"]).reshape(DI, 1)),
        "dvec": c(np.asarray(inp[f"L0_attn_D_{sfx}"]).reshape(DI, 1)),
    }


def _prep_B_shared(inp):
    c = np.ascontiguousarray
    gu = np.asarray(inp["L1_ffn_gate_up_weight"])
    dn = np.asarray(inp["L1_ffn_down_weight"])
    return {
        "attw": c(np.asarray(inp["L1_attn_weight"])),
        "routT": c(np.asarray(inp["L1_ffn_router_weight"]).T),
        "nw": c(np.asarray(inp["L1_ffn_norm_weight"]).reshape(D, 1)),
        "guT": c(gu.transpose(0, 2, 1)),
        "dnT": c(dn.transpose(0, 2, 1)),
        "alpha": c(np.asarray(inp["L1_ffn_residual_alpha"], dtype=np.float32).reshape(1, 1)),
        "idbf": np.eye(128, dtype=BF),
        "idf32": np.eye(128, dtype=np.float32),
    }


def kernel(**inputs):
    if "F" not in _CACHE:
        _CACHE["F"] = build_fused(paranoid=PARANOID)
    ncF = _CACHE["F"]
    sharedA = _prep_A_inputs(inputs)
    sharedB = _prep_B_shared(inputs)
    dirs = [_prep_A_dir(inputs, 0), _prep_A_dir(inputs, 1)]
    jrev = np.ascontiguousarray(np.eye(128, dtype=BF)[::-1])
    xfull = np.asarray(inputs["L0_attn_hidden_states"])
    in_maps = []
    for c in range(8):
        b, d = c // 2, c % 2
        m = dict(sharedA)
        m.update(sharedB)
        m.update(dirs[d])
        m["jrevh"] = jrev
        xb = xfull[b] if d == 0 else xfull[b][::-1]
        m["x"] = np.ascontiguousarray(xb)
        in_maps.append(m)
    res = run_bass_kernel_spmd(ncF, in_maps, core_ids=list(range(8)))
    out = np.empty((NB, L, D), np.float32)
    for c in range(8):
        b, d = c // 2, c % 2
        o = res.results[c]["outT"].T
        if d == 0:
            out[b, :LH] = o
        else:
            out[b, LH:] = o[::-1]
    return out


# version 14 (final)
# speedup vs baseline: 1.5765x; 1.5765x over previous
import numpy as np
import ml_dtypes
from contextlib import ExitStack

import concourse.bass as bass
import concourse.mybir as mybir
from concourse.bass_utils import run_bass_kernel_spmd

F32 = mybir.dt.float32
BF16 = mybir.dt.bfloat16
AF = mybir.ActivationFunctionType
ALU = mybir.AluOpType
AX = mybir.AxisListType

D = 1024
DI = 2048
L = 4096
NST = 16
RK = 64
DFF = 4096
LH = L // 2
NB = 4
NHEAD = 16
HS = 64
NEXP = 8
HID = 2816
BF = ml_dtypes.bfloat16

ENGS = ("sync", "scalar", "vector", "tensor", "gpsimd")


class Sched:

    SERIAL = ("vector", "scalar", "gpsimd")

    def __init__(self, nc, ctx, paranoid=False, paranoid_phases=None):
        self.nc = nc
        self.sems = {}
        for e in ENGS:
            self.sems[(e, False)] = ctx.enter_context(nc.semaphore(f"S_{e}"))
            self.sems[(e, True)] = ctx.enter_context(nc.semaphore(f"S_{e}_d"))
        self.ops = {e: [] for e in ENGS}
        self.counts = {(e, c): 0 for e in ENGS for c in (False, True)}
        self.nphase = 0
        self.last_comp = {}
        self.paranoid_phases = paranoid_phases
        if paranoid_phases is not None:
            paranoid = 0 in paranoid_phases
        self.paranoid = paranoid
        self.last_any = None
        self.serialize_eng = set()
        self.last_eng = {}
        self.barrier = True
        self.join_prev = {}

    @staticmethod
    def _skip_dep(pe, prec, eng, rec):
        if pe != eng:
            return False
        if pe == "tensor":
            return True
        if prec["dma"] and rec["dma"]:
            return True
        return False

    def add(self, eng, emit_fn, deps=(), dma=False, inc=None):
        deps = [d for d in deps if d is not None]
        if self.barrier and not self.ops[eng] and self.join_prev:
            deps.extend(t for t in self.join_prev.values() if t is not None)
        if self.paranoid and self.last_any is not None:
            deps.append(self.last_any)
        if eng in self.serialize_eng and eng in self.last_eng:
            deps.append(self.last_eng[eng])
        if eng in self.SERIAL and not dma and eng in self.last_comp:
            deps.append(self.last_comp[eng])
        rec = {"emit": emit_fn, "deps": deps, "inc": inc,
               "dma": dma, "eng": eng, "tagged": True, "count_after": None}
        self.ops[eng].append(rec)
        tok = (eng, rec)
        if eng in self.SERIAL and not dma:
            self.last_comp[eng] = tok
        self.last_any = tok
        self.last_eng[eng] = tok
        return tok

    def dma(self, eng, out, in_, deps=(), slow=False):
        if slow:
            return self.add(eng, lambda q: q.dma_start(
                out=out, in_=in_, allow_slow_non_contiguous=True), deps=deps, dma=True)
        return self.add(eng, lambda q: q.dma_start(out=out, in_=in_),
                        deps=deps, dma=True)

    def set_paranoid(self, flag):
        self.paranoid = flag

    def emit_phase(self, name):
        nc = self.nc
        for e in ENGS:
            for rec in self.ops[e]:
                for (pe, prec) in rec["deps"]:
                    if self._skip_dep(pe, prec, e, rec):
                        continue
                    prec["tagged"] = True
        for e in ENGS:
            cc = self.counts[(e, False)]
            cd = self.counts[(e, True)]
            for rec in self.ops[e]:
                if rec["tagged"]:
                    if rec["dma"]:
                        cd += 16
                    else:
                        cc += 1
                rec["count_after"] = cd if rec["dma"] else cc
            self.counts[(e, False)] = cc
            self.counts[(e, True)] = cd
        pending = {e: self.ops[e] for e in ENGS}
        self.ops = {e: [] for e in ENGS}
        sems = self.sems
        ptr = {e: 0 for e in ENGS}
        done_sem = {}
        for e in ENGS:
            for c in (False, True):
                done_sem[(e, c)] = self.counts[(e, c)] - sum(
                    (16 if r["dma"] else 1) for r in pending[e]
                    if r["tagged"] and r["dma"] == c)
        progress = True
        while progress:
            progress = False
            for e in ENGS:
                while ptr[e] < len(pending[e]):
                    rec = pending[e][ptr[e]]
                    ok = True
                    for (pe, prec) in rec["deps"]:
                        if pe == e:
                            continue
                        if prec["count_after"] is not None and \
                                done_sem.get((pe, prec["dma"]), 0) < prec["count_after"]:
                            ok = False
                            break
                    if not ok:
                        break
                    if rec["tagged"]:
                        done_sem[(e, rec["dma"])] += 16 if rec["dma"] else 1
                    ptr[e] += 1
                    progress = True
        stuck = {e: (ptr[e], len(pending[e])) for e in ENGS if ptr[e] < len(pending[e])}
        if stuck:
            for e, (p, n) in stuck.items():
                rec = pending[e][p]
                info = []
                for (pe, prec) in rec["deps"]:
                    info.append((pe, prec["dma"], prec["count_after"],
                                 done_sem.get((pe, prec["dma"]))))
                print(f"DEADLOCK phase={name} eng={e} op#{p}/{n} deps={info}")
            raise RuntimeError(f"phase {name}: deadlock {stuck}")

        with nc.Block(f"ph{self.nphase}_{name}") as blk:
            for e in ENGS:
                if not pending[e]:
                    continue

                def body(eng_obj, recs=pending[e], ename=e):
                    hiwater = {}
                    for rec in recs:
                        for (pe, prec) in rec["deps"]:
                            if Sched._skip_dep(pe, prec, ename, rec):
                                continue
                            if prec["count_after"] is None:
                                continue
                            key = (pe, prec["dma"])
                            v = prec["count_after"]
                            if hiwater.get(key, -1) < v:
                                eng_obj.wait_ge(sems[key], v)
                                hiwater[key] = v
                        ins = rec["emit"](eng_obj)
                        if rec["tagged"]:
                            if rec.get("inc") == "bare":
                                ins.then_inc(sems[(ename, rec["dma"])])
                            else:
                                ins.then_inc(sems[(ename, rec["dma"])],
                                             16 if rec["dma"] else 1)

                getattr(blk, e)(body)
        self.nphase += 1
        if self.barrier:
            self.join_prev = dict(self.last_eng)
        if self.paranoid_phases is not None:
            self.paranoid = self.nphase in self.paranoid_phases



def emit_A(nc, S, ctx, sbt, psA, psB, io, cm):
    x = io["x"]
    winT = io["winT"]
    wxT = io["wxT"]
    wdtT = io["wdtT"]
    convw = io["convw"]
    convb = io["convb"]
    alog = io["alog"]
    bdt = io["bdt"]
    dvec = io["dvec"]
    woutT = io["woutT"]
    wupT = io["wupT"]
    wdnT = io["wdnT"]
    idbf = io["idbf"]
    idf32 = io["idf32"]
    jrevh = io["jrevh"]
    SEND = io["SEND"]
    xssm_d = io["xssm_d"]
    sz_d = io["sz_d"]
    v_d = io["v_d"]
    bc_d = io["bc_d"]
    sq_d = io["sq_d"]
    h_d = io["h_d"]

    identbf = cm["identbf"]
    identf = cm["identf"]
    onesf_row = cm["onesf_row"]
    onesf_col = cm["onesf_col"]
    onesbf_col = cm["onesbf_col"]

    if True:
        pA_outer = ExitStack()
        consts = sbt("consts", [128, 384], F32, pA_outer)
        carry = sbt("carry", [128, 256], F32, pA_outer)
        dtr_bf = sbt("dtr_bf", [64, L], BF16, pA_outer)
        p0s = ExitStack()
        Abuf = sbt("Abuf", [128, 256], F32, p0s)

        t_cw = S.dma("sync", consts[:, 0:64], convw.rearrange("(t p) k -> p t k", p=128))
        t_cb = S.dma("sync", consts[:, 64:80], convb.rearrange("(t p) o -> p t o", p=128), slow=True)
        t_bd = S.dma("sync", consts[:, 80:96], bdt.rearrange("(t p) o -> p t o", p=128), slow=True)
        t_dv = S.dma("sync", consts[:, 96:112], dvec.rearrange("(t p) o -> p t o", p=128), slow=True)
        t_al = S.dma("sync", consts[:, 128:384], alog.rearrange("(t p) n -> p t n", p=128))
        t_ib = S.dma("sync", identbf[:], idbf[:, :])
        t_if = S.dma("sync", identf[:], idf32[:, :])
        t_o1 = S.add("gpsimd", lambda g: g.memset(onesf_row[:], 1.0))
        t_o2 = S.add("gpsimd", lambda g: g.memset(onesf_col[:], 1.0))
        t_o3 = S.add("gpsimd", lambda g: g.memset(onesbf_col[:], 1.0))
        t_o4 = S.add("gpsimd", lambda g: g.memset(carry[:], 0.0))
        t_ae = S.add("scalar", lambda s: s.activation(Abuf[:], consts[:, 128:384], AF.Exp),
                     deps=[t_al])
        t_an = S.add("vector", lambda v: v.tensor_scalar_mul(consts[:, 128:384], Abuf[:], -1.0),
                     deps=[t_ae])
        S.emit_phase("consts")
        p0s.close()

        p1 = ExitStack()
        p1x = ExitStack()
        xdbl = sbt("xdbl", [96, L], F32, p1)
        sumsq_row = sbt("sumsq_row", [1, LH], F32, p1)
        rcp_row = sumsq_row
        rcp2_row = sbt("rcp2_row", [1, LH], F32, p1)
        hss_row = sbt("hss_row", [1, LH], F32, p1)
        rms_row = hss_row
        xnT = sbt("xnT", [128, 8 * L], BF16, p1x, side="right")
        p1a = ExitStack()
        xt = [sbt(f"xt{j}", [128, D], F32, p1a) for j in range(2)]
        xsq = [sbt(f"xsq{j}", [128, D], F32, p1a) for j in range(2)]
        xnt = sbt("xnt", [128, D], BF16, p1a)
        stat = sbt("stat", [128, 32], F32, p1a)
        sqcols = sbt("sqcols", [128, 32], F32, p1a)
        sqtr = sbt("sqtr", [32, 128], F32, p1a)

        tok_norm = [None] * 32
        tok_tr_last = [None] * 32
        tok_rsq = [None] * 32
        tok_cp = {}
        tffn = None
        for i in range(32):
            sb = (i % 2) * 16
            deps = [tok_norm[i - 2]] if i >= 2 else []
            tl = S.dma("sync", xt[i % 2][:], x[i * 128:(i + 1) * 128, :], deps=deps)
            tsq = S.add("scalar", lambda s, i=i: s.activation(xsq[i % 2][:], xt[i % 2][:],
                                                              AF.Square),
                        deps=[tl] + ([tok_rsq[i - 2]] if i >= 2 else []))
            S.add("vector", lambda v, i=i, sb=sb: v.reduce_sum(
                stat[:, sb + 0:sb + 1], xt[i % 2][:], axis=AX.X), deps=[tl])
            tok_rsq[i] = S.add("vector", lambda v, i=i, sb=sb: v.reduce_sum(
                stat[:, sb + 1:sb + 2], xsq[i % 2][:], axis=AX.X), deps=[tsq])
            S.add("vector", lambda v, sb=sb: v.tensor_scalar_mul(
                stat[:, sb + 2:sb + 3], stat[:, sb + 0:sb + 1], 1.0 / D))
            S.add("vector", lambda v, sb=sb: v.tensor_scalar_mul(
                stat[:, sb + 3:sb + 4], stat[:, sb + 1:sb + 2], 1.0 / D))
            S.add("vector", lambda v, sb=sb: v.tensor_mul(
                stat[:, sb + 4:sb + 5], stat[:, sb + 2:sb + 3], stat[:, sb + 2:sb + 3]))
            S.add("vector", lambda v, sb=sb: v.tensor_sub(
                stat[:, sb + 5:sb + 6], stat[:, sb + 3:sb + 4], stat[:, sb + 4:sb + 5]))
            tvp = S.add("vector", lambda v, sb=sb: v.tensor_scalar_add(
                stat[:, sb + 6:sb + 7], stat[:, sb + 5:sb + 6], 1e-5))
            tsqr = S.add("scalar", lambda s, sb=sb: s.activation(
                stat[:, sb + 7:sb + 8], stat[:, sb + 6:sb + 7], AF.Sqrt), deps=[tvp])
            trs = S.add("vector", lambda v, sb=sb: v.reciprocal(
                stat[:, sb + 8:sb + 9], stat[:, sb + 7:sb + 8]), deps=[tsqr])
            tnb = S.add("vector", lambda v, sb=sb: v.tensor_scalar(
                stat[:, sb + 9:sb + 10], stat[:, sb + 2:sb + 3], stat[:, sb + 8:sb + 9],
                -1.0, ALU.mult, ALU.mult))
            ndeps = [tnb]
            if i >= 1:
                ndeps.append(tok_tr_last[i - 1])
            tok_norm[i] = S.add("scalar", lambda s, i=i, sb=sb: s.activation(
                xnt[:], xt[i % 2][:], AF.Identity, bias=stat[:, sb + 9:sb + 10],
                scale=stat[:, sb + 8:sb + 9]), deps=ndeps)
            S.add("vector", lambda v, sb=sb: v.tensor_mul(
                stat[:, sb + 10:sb + 11], stat[:, sb + 5:sb + 6], stat[:, sb + 8:sb + 9]))
            S.add("vector", lambda v, sb=sb: v.tensor_mul(
                stat[:, sb + 11:sb + 12], stat[:, sb + 10:sb + 11], stat[:, sb + 8:sb + 9]))
            tffn = S.add("vector", lambda v, i=i, sb=sb: v.tensor_scalar_mul(
                sqcols[:, i:i + 1], stat[:, sb + 11:sb + 12], float(D)))
            for k in range(8):
                war = [tok_cp[i * 8 + k - 2]] if (i * 8 + k) >= 2 else []
                ttr = S.add("tensor", lambda t, i=i, k=k: t.transpose(
                    psB[:, (k % 2) * 1024:(k % 2) * 1024 + 128],
                    xnt[:, k * 128:(k + 1) * 128], identbf[:]),
                    deps=[tok_norm[i], t_ib] + war)
                tok_cp[i * 8 + k] = S.add("vector", lambda v, i=i, k=k: v.tensor_copy(
                    xnT[:, k * L + i * 128: k * L + (i + 1) * 128],
                    psB[:, (k % 2) * 1024:(k % 2) * 1024 + 128]), deps=[ttr])
                tok_tr_last[i] = ttr
        tfl = S.add("tensor", lambda t: t.transpose(psA[0:32, 0:128], sqcols[:, 0:32],
                                                    identf[:]), deps=[tffn, t_if])
        tfc = S.add("vector", lambda v: v.tensor_copy(sqtr[:], psA[0:32, 0:128]), deps=[tfl])
        tq1 = S.dma("gpsimd", sq_d[:, :], sqtr[:], deps=[tfc])
        tq2 = S.dma("gpsimd", sumsq_row[0:1, :],
                    sq_d[0:16, :].rearrange("t p -> (t p)").unsqueeze(0), deps=[tq1])
        S.emit_phase("ln")
        p1a.close()

        wtern = sbt("wtern", [128, 32768], BF16, p1)
        p2w = ExitStack()
        wx_bf = sbt("wx_bf", [128, 16 * 96], BF16, p2w)
        window = sbt("window", [128, 515], F32, p2w)
        tails = sbt("tails", [128, 48], F32, p2w)
        acc = [sbt(f"acc{j}", [128, 512], F32, p2w) for j in range(2)]
        xs_chunk = sbt("xs_chunk", [128, 16 * 512], BF16, p2w)
        sz_t = [sbt(f"sz_t{j}", [128, 512], BF16, p2w) for j in range(2)]

        tcast = []
        for kt in range(8):
            tcast.append(S.dma("gpsimd", wtern[:, kt * 4096:(kt + 1) * 4096],
                               winT[kt * 128:(kt + 1) * 128, :]))
        tcwx = S.dma("gpsimd", wx_bf[:], wxT.rearrange("(t p) e -> p t e", p=128))
        tz = S.add("vector", lambda v: v.memset(tails[:], 0.0))

        tok_mm = {}
        tok_evx = {}
        tok_evz = {}
        tok_silu = {}
        tok_wr = {}
        tok_zwr = {}
        tok_convd = {}
        tok_xde = None
        prev_silu = None
        for ch in range(8):
            for mt in range(32):
                slot = (mt % 2) * 512
                gi = ch * 32 + mt
                war = []
                if gi >= 2:
                    pch, pmt = divmod(gi - 2, 32)
                    war = [tok_evx[(pch, pmt)] if pmt < 16 else tok_evz[(pch, pmt - 16)]]

                def mmfn(t, ch=ch, mt=mt, slot=slot):
                    for kt in range(8):
                        ins = t.matmul(psA[:, slot:slot + 512],
                                       wtern[:, kt * 4096 + mt * 128:kt * 4096 + mt * 128 + 128],
                                       xnT[:, kt * L + ch * 512:kt * L + (ch + 1) * 512],
                                       start=(kt == 0), stop=(kt == 7))
                    return ins
                tok_mm[(ch, mt)] = S.add("tensor", mmfn,
                                         deps=[tcast[7], tok_cp[255]] + war)
                if mt < 16:
                    tt = mt
                    tin_deps = [tz] if ch == 0 else [tok_convd[(ch - 1, tt)]]
                    S.add("vector", lambda v, tt=tt: v.tensor_copy(
                        window[:, 0:3], tails[:, tt * 3:tt * 3 + 3]), deps=tin_deps)
                    tok_evx[(ch, tt)] = S.add("vector", lambda v, slot=slot: v.tensor_copy(
                        window[:, 3:515], psA[:, slot:slot + 512]), deps=[tok_mm[(ch, mt)]])
                    S.add("vector", lambda v, tt=tt: v.tensor_copy(
                        tails[:, tt * 3:tt * 3 + 3], window[:, 512:515]))
                    c0_deps = [t_cw] + ([prev_silu] if prev_silu else [])
                    S.add("vector", lambda v, tt=tt: v.tensor_scalar_mul(
                        acc[0][:], window[:, 0:512], consts[:, 4 * tt:4 * tt + 1]),
                        deps=c0_deps)
                    S.add("vector", lambda v, tt=tt: v.scalar_tensor_tensor(
                        acc[1][:], window[:, 1:513], consts[:, 4 * tt + 1:4 * tt + 2],
                        acc[0][:], ALU.mult, ALU.add))
                    S.add("vector", lambda v, tt=tt: v.scalar_tensor_tensor(
                        acc[0][:], window[:, 2:514], consts[:, 4 * tt + 2:4 * tt + 3],
                        acc[1][:], ALU.mult, ALU.add))
                    tok_convd[(ch, tt)] = S.add("vector", lambda v, tt=tt: v.scalar_tensor_tensor(
                        acc[1][:], window[:, 3:515], consts[:, 4 * tt + 3:4 * tt + 4],
                        acc[0][:], ALU.mult, ALU.add))
                    sdeps = [tok_convd[(ch, tt)], t_cb]
                    if ch > 0:
                        sdeps.append(tok_wr[(ch - 1, tt)])
                    tok_silu[(ch, tt)] = S.add("scalar", lambda s, tt=tt: s.activation(
                        xs_chunk[:, tt * 512:(tt + 1) * 512], acc[1][:], AF.Silu,
                        bias=consts[:, 64 + tt:64 + tt + 1]), deps=sdeps)
                    prev_silu = tok_silu[(ch, tt)]
                    tok_wr[(ch, tt)] = S.dma(
                        "sync", xssm_d[tt * 128:(tt + 1) * 128, ch * 512:(ch + 1) * 512],
                        xs_chunk[:, tt * 512:(tt + 1) * 512], deps=[tok_silu[(ch, tt)]])
                else:
                    tzi = mt - 16
                    zslot = tzi % 2
                    gzi = ch * 16 + tzi
                    zdeps = [tok_mm[(ch, mt)]]
                    if gzi >= 2:
                        pch, ptz = divmod(gzi - 2, 16)
                        zdeps.append(tok_zwr[(pch, ptz)])
                    tok_evz[(ch, tzi)] = S.add("scalar", lambda s, zslot=zslot, slot=slot:
                                               s.activation(sz_t[zslot][:],
                                                            psA[:, slot:slot + 512], AF.Silu),
                                               deps=zdeps)
                    tok_zwr[(ch, tzi)] = S.dma(
                        "scalar", sz_d[tzi * 128:(tzi + 1) * 128, ch * 512:(ch + 1) * 512],
                        sz_t[zslot][:], deps=[tok_evz[(ch, tzi)]])

            def xdblfn(t, ch=ch):
                for tt in range(16):
                    ins = t.matmul(psA[0:96, 1024:1536], wx_bf[:, tt * 96:(tt + 1) * 96],
                                   xs_chunk[:, tt * 512:(tt + 1) * 512],
                                   start=(tt == 0), stop=(tt == 15))
                return ins
            xd_deps = [tcwx] + [tok_silu[(ch, tt)] for tt in range(16)]
            if tok_xde is not None:
                xd_deps.append(tok_xde)
            tok_xd = S.add("tensor", xdblfn, deps=xd_deps)
            tok_xde = S.add("vector", lambda v, ch=ch: v.tensor_copy(
                xdbl[:, ch * 512:(ch + 1) * 512], psA[0:96, 1024:1536]), deps=[tok_xd])
        tbc = S.dma("gpsimd", bc_d[:, :], xdbl[64:96, :], deps=[tok_xde])
        S.emit_phase("xz")
        p2w.close()

        stage = [sbt(f"stage{j}", [128, 2048], F32, p1) for j in range(2)]
        abscols = sbt("abscols", [128, 34], F32, p1)
        trow = sbt("trow", [1, 2], F32, p1)
        thr_col = sbt("thr_col", [128, 2], F32, p1)
        posb = sbt("posb", [128, 512], F32, p1)
        negb = sbt("negb", [128, 512], F32, p1)
        p3w = ExitStack()
        rbc = sbt("rbc", [128, 512], F32, p3w)
        tfp = [sbt("tfp0", [128, 512], F32, p3w)] * 2
        hbuf = [sbt(f"hbuf{j}", [128, 512], BF16, p3w) for j in range(2)]

        def ternarize(srcslice_fn, npiece, pcols, wslice_fn, nelem):
            tred = []
            for kp in range(npiece):
                deps = [tred[kp - 2]] if kp >= 2 else []
                tl = S.dma("sync", stage[kp % 2][:, 0:pcols], srcslice_fn(kp), deps=deps)
                tred.append(S.add("vector", lambda v, kp=kp: v.reduce_sum(
                    abscols[:, kp:kp + 1], stage[kp % 2][:, 0:pcols], axis=AX.X,
                    apply_absolute_value=True), deps=[tl]))
            tacc = S.add("vector", lambda v: v.tensor_add(
                abscols[:, 32:33], abscols[:, 0:1], abscols[:, 1:2]), deps=[tred[1]])
            for j in range(2, npiece):
                tacc = S.add("vector", lambda v, j=j: v.tensor_add(
                    abscols[:, 32:33], abscols[:, 32:33], abscols[:, j:j + 1]),
                    deps=[tred[j]])
            tmm1 = S.add("tensor", lambda t: t.matmul(
                psA[0:1, 1536:1537], onesf_col[:, 0:1], abscols[:, 32:33],
                start=True, stop=True), deps=[tacc, t_o2])
            S.add("vector", lambda v: v.tensor_copy(trow[0:1, 0:1], psA[0:1, 1536:1537]),
                  deps=[tmm1])
            tth = S.add("vector", lambda v: v.tensor_scalar_mul(
                trow[0:1, 1:2], trow[0:1, 0:1], 0.5 / nelem))
            tmm2 = S.add("tensor", lambda t: t.matmul(
                psA[:, 1792:1793], onesf_row[0:1, :], trow[0:1, 1:2],
                start=True, stop=True), deps=[tth, t_o1])
            S.add("vector", lambda v: v.tensor_copy(thr_col[:, 0:1], psA[:, 1792:1793]),
                  deps=[tmm2])
            tng = S.add("vector", lambda v: v.tensor_scalar_mul(
                thr_col[:, 1:2], thr_col[:, 0:1], -1.0))
            tw = []
            for kp in range(npiece):
                deps = [tng] + ([tw[kp - 2]] if kp >= 2 else [])
                tl = S.dma("sync", stage[kp % 2][:, 0:pcols], srcslice_fn(kp), deps=deps)
                tlast = None
                for so in range(0, pcols, 512):
                    w = min(512, pcols - so)
                    S.add("vector", lambda v, kp=kp, so=so, w=w: v.tensor_single_scalar(
                        posb[:, 0:w], stage[kp % 2][:, so:so + w], thr_col[:, 0:1],
                        ALU.is_gt), deps=[tl])
                    S.add("vector", lambda v, kp=kp, so=so, w=w: v.tensor_single_scalar(
                        negb[:, 0:w], stage[kp % 2][:, so:so + w], thr_col[:, 1:2],
                        ALU.is_lt))
                    tlast = S.add("vector", lambda v, kp=kp, so=so, w=w: v.tensor_sub(
                        wslice_fn(kp)[:, so:so + w], posb[:, 0:w], negb[:, 0:w]))
                tw.append(tlast)
            return tw

        twup = ternarize(
            lambda kp: wupT[(kp // 2) * 128:(kp // 2 + 1) * 128,
                            (kp % 2) * 2048:(kp % 2 + 1) * 2048],
            16, 2048,
            lambda kp: wtern[:, (kp // 2) * 4096 + (kp % 2) * 2048:
                             (kp // 2) * 4096 + (kp % 2 + 1) * 2048],
            1024.0 * 4096.0)

        tsqx = S.add("scalar", lambda s: s.activation(
            rms_row[0:1, :], sumsq_row[0:1, :], AF.Sqrt, scale=1.0 / D), deps=[tq2])
        S.add("vector", lambda v: v.tensor_scalar_add(rms_row[0:1, :], rms_row[0:1, :],
                                                      1e-8), deps=[tsqx])
        trcp = S.add("vector", lambda v: v.reciprocal(rcp_row[0:1, :], rms_row[0:1, :]))

        tok_gelu = {}
        tok_t1 = {}
        tok_hw = {}
        trbe = None
        for ch in range(4):
            trb = S.add("tensor", lambda t, ch=ch: t.matmul(
                psA[:, 2048:2560], onesf_row[0:1, :], rcp_row[0:1, ch * 512:(ch + 1) * 512],
                start=True, stop=True), deps=[trcp, t_o1, trbe])
            trbe = S.add("vector", lambda v: v.tensor_copy(rbc[:], psA[:, 2048:2560]),
                         deps=[trb])
            for mt in range(32):
                gi = ch * 32 + mt
                war = [tok_t1[gi - 2]] if gi >= 2 else []

                def upfn(t, ch=ch, mt=mt):
                    for kt in range(8):
                        ins = t.matmul(psA[:, (mt % 2) * 512:(mt % 2) * 512 + 512],
                                       wtern[:, kt * 4096 + mt * 128:kt * 4096 + mt * 128 + 128],
                                       xnT[:, kt * L + ch * 512:kt * L + (ch + 1) * 512],
                                       start=(kt == 0), stop=(kt == 7))
                    return ins
                tmm = S.add("tensor", upfn, deps=[twup[15], trbe] + war)
                t1_deps = [tmm] + ([tok_gelu[gi - 1]] if gi >= 1 else [])
                tok_t1[gi] = S.add("vector", lambda v, mt=mt, gi=gi: v.tensor_mul(
                    tfp[gi % 2][:], psA[:, (mt % 2) * 512:(mt % 2) * 512 + 512], rbc[:]),
                    deps=t1_deps)
                g_deps = [tok_t1[gi]] + ([tok_hw[gi - 2]] if gi >= 2 else [])
                tok_gelu[gi] = S.add("scalar", lambda s, gi=gi: s.activation(
                    hbuf[gi % 2][:], tfp[gi % 2][:], AF.Gelu), deps=g_deps)
                tok_hw[gi] = S.dma("gpsimd",
                                   h_d[mt * 128:(mt + 1) * 128, ch * 512:(ch + 1) * 512],
                                   hbuf[gi % 2][:], deps=[tok_gelu[gi]])
        S.emit_phase("up")
        p1x.close()
        p3w.close()

        p3s = ExitStack()
        hld2 = [sbt(f"hld2_{j}", [128, 512], BF16, p3s) for j in range(2)]
        sq2 = [sbt(f"sq2_{j}", [128, 512], BF16, p3s) for j in range(2)]
        tok_hss = {}
        tsq_prev = {}
        for ch in range(4):
            tmm_prev = None
            for kf in range(32):
                gk = ch * 32 + kf
                ldeps = [tok_hw[gk]]
                if gk >= 2:
                    ldeps.append(tsq_prev[gk - 2])
                tl = S.dma("sync", hld2[kf % 2][:],
                           h_d[kf * 128:(kf + 1) * 128, ch * 512:(ch + 1) * 512], deps=ldeps)
                sdeps = [tl]
                if gk >= 2:
                    sdeps.append(tok_hss.get("mm%d" % (gk - 2)))
                tsq_prev[gk] = S.add("vector", lambda v, kf=kf: v.tensor_mul(
                    sq2[kf % 2][:], hld2[kf % 2][:], hld2[kf % 2][:]), deps=sdeps)
                mmdeps = [tsq_prev[gk], t_o3]
                if kf == 0 and ch > 0:
                    mmdeps.append(tok_hss[ch - 1])
                tmm_prev = S.add("tensor", lambda t, kf=kf: t.matmul(
                    psA[0:1, 2560:3072], onesbf_col[:, 0:1], sq2[kf % 2][:],
                    start=(kf == 0), stop=(kf == 31)), deps=mmdeps)
                tok_hss["mm%d" % gk] = tmm_prev
            tok_hss[ch] = S.add("vector", lambda v, ch=ch: v.tensor_copy(
                hss_row[0:1, ch * 512:(ch + 1) * 512], psA[0:1, 2560:3072]),
                deps=[tmm_prev])
        ths = S.add("scalar", lambda s: s.activation(
            rcp2_row[0:1, :], hss_row[0:1, :], AF.Sqrt, scale=1.0 / DFF),
            deps=[tok_hss[0], tok_hss[1], tok_hss[2], tok_hss[3]])
        S.add("vector", lambda v: v.tensor_scalar_add(rcp2_row[0:1, :], rcp2_row[0:1, :],
                                                      1e-8), deps=[ths])
        tr2 = S.add("vector", lambda v: v.reciprocal(rcp2_row[0:1, :], rcp2_row[0:1, :]))
        S.emit_phase("hss")
        p3s.close()

        hldb = sbt("hldb", [128, 8 * 512], BF16, p1)
        hnb = sbt("hnb", [128, 8 * 512], BF16, p1)
        rbc2 = sbt("rbc2", [128, 512], F32, p1)
        gtile = [sbt(f"gtile{j}", [128, 512], BF16, p1) for j in range(2)]

        twdn = ternarize(
            lambda kp: wdnT[kp * 128:(kp + 1) * 128, :], 32, 1024,
            lambda kp: wtern[:, kp * 1024:(kp + 1) * 1024], 4096.0 * 1024.0)

        tok_hn = {}
        tok_gw = {}
        tok_gs = {}
        tok_dmm = {}
        last_dmm = None
        trbe2 = None
        for ch in range(4):
            trb2 = S.add("tensor", lambda t, ch=ch: t.matmul(
                psA[:, 2048:2560], onesf_row[0:1, :], rcp2_row[0:1, ch * 512:(ch + 1) * 512],
                start=True, stop=True), deps=[tr2, t_o1, trbe2])
            trbe2 = S.add("vector", lambda v: v.tensor_copy(rbc2[:], psA[:, 2048:2560]),
                          deps=[trb2])
            for mtg in range(2):
                for kfh in range(4):
                    for j in range(8):
                        kf = kfh * 8 + j
                        deps = [tok_hw[ch * 32 + kf], trbe2]
                        if last_dmm is not None:
                            deps.append(last_dmm)
                        tl = S.dma("scalar", hldb[:, j * 512:(j + 1) * 512],
                                   h_d[kf * 128:(kf + 1) * 128, ch * 512:(ch + 1) * 512],
                                   deps=deps)
                        tok_hn[j] = S.add("vector", lambda v, j=j: v.tensor_mul(
                            hnb[:, j * 512:(j + 1) * 512], hldb[:, j * 512:(j + 1) * 512],
                            rbc2[:]), deps=[tl])
                    for mi in range(4):
                        mtd = mtg * 4 + mi
                        gi = ch * 8 + mtd
                        deps = [twdn[31]] + [tok_hn[j] for j in range(8)]
                        if kfh == 0:
                            if (ch, mtg) != (0, 0):
                                pch, pmtg = divmod(ch * 2 + mtg - 1, 2)
                                deps.append(tok_gs[pch * 8 + pmtg * 4 + mi])

                        def dnfn(t, mtd=mtd, mi=mi, kfh=kfh):
                            for j in range(8):
                                kf = kfh * 8 + j
                                ins = t.matmul(
                                    psA[:, mi * 512:(mi + 1) * 512],
                                    wtern[:, kf * 1024 + mtd * 128:kf * 1024 + mtd * 128 + 128],
                                    hnb[:, j * 512:(j + 1) * 512],
                                    start=(kfh == 0 and j == 0),
                                    stop=(kfh == 3 and j == 7))
                            return ins
                        tok_dmm[(ch, mtd, kfh)] = S.add("tensor", dnfn, deps=deps)
                    last_dmm = tok_dmm[(ch, mtg * 4 + 3, kfh)]
                for mi in range(4):
                    mtd = mtg * 4 + mi
                    gi = ch * 8 + mtd
                    gs_deps = [tok_dmm[(ch, mtd, 3)]]
                    if gi >= 2:
                        gs_deps.append(tok_gw[gi - 2])
                    tok_gs[gi] = S.add("scalar", lambda s, mi=mi, gi=gi: s.activation(
                        gtile[gi % 2][:], psA[:, mi * 512:(mi + 1) * 512],
                        AF.Silu), deps=gs_deps)
                    tok_gw[gi] = S.dma(
                        "sync", SEND[mtd * 128:(mtd + 1) * 128,
                                     L + ch * 512:L + (ch + 1) * 512],
                        gtile[gi % 2][:], deps=[tok_gs[gi]])
        tcdr = S.add("vector", lambda v: v.tensor_copy(dtr_bf[:], xdbl[0:64, :]),
                     deps=[tbc])
        S.emit_phase("down")

        p1.close()

        p4 = ExitStack()
        bcb = sbt("bcb", [128, 32 * 1024], F32, p4)
        wdt_bf = sbt("wdt_bf", [64, DI], BF16, p4)
        xs4 = [sbt(f"xs4_{j}", [128, 1024], BF16, p4) for j in range(2)]
        sz4 = [sbt(f"sz4_{j}", [128, 1024], BF16, p4) for j in range(2)]
        ub4 = sbt("ub4", [128, 512], F32, p4)
        ab4 = sbt("ab4", [128, 512], F32, p4)
        eb4 = sbt("eb4", [128, 512], F32, p4)
        lb4 = sbt("lb4", [128, 512], F32, p4)
        rl4 = sbt("rl4", [128, 512], F32, p4)
        dtb = sbt("dtb", [128, 1024], F32, p4)
        dtx = sbt("dtx", [128, 1024], F32, p4)
        dab = [sbt(f"dab{j}", [128, 1024], F32, p4) for j in range(2)]
        dbx = [sbt(f"dbx{j}", [128, 1024], F32, p4) for j in range(2)]
        hb4 = [sbt(f"hb4_{j}", [128, 1024], F32, p4) for j in range(2)]
        tmp4 = [sbt(f"tmp4_{j}", [128, 1024], F32, p4) for j in range(2)]
        vt4 = sbt("vt4", [128, 512], F32, p4)
        vb4 = [sbt(f"vb4_{j}", [128, 1024], BF16, p4) for j in range(2)]

        tcdw = S.dma("gpsimd", wdt_bf[:], wdtT[:, :])

        tok_bc = {}
        tok_dbx = {}
        tok_tts = {}
        tok_tmp = {}
        tok_ymm = {}
        tok_vb = {}
        tok_vw = {}
        tok_xl = {}
        tok_zl = {}
        tok_dtx = {}
        tok_u = {}
        tok_dalast = {}
        te1_prev = None
        tdtb_prev = None
        for lq in range(4):
            for n in range(32):
                deps = [tbc]
                if lq > 0:
                    deps += [tok_tmp[(lq - 1, 15, 14)], tok_tmp[(lq - 1, 15, 15)],
                             tok_dbx[(lq - 1, 15, 15)], tok_dbx[(lq - 1, 15, 14)]]
                q = "sync" if n % 2 == 0 else "scalar"
                tok_bc[(lq, n)] = S.dma(
                    q, bcb[:, n * 1024:(n + 1) * 1024],
                    bc_d[n:n + 1, lq * 1024:(lq + 1) * 1024].partition_broadcast(128),
                    deps=deps)
            for t in range(16):
                idx = lq * 16 + t
                ldeps = [tok_vw[idx - 2]] if idx >= 2 else []
                tok_xl[idx] = S.dma("sync", xs4[t % 2][:],
                                    xssm_d[t * 128:(t + 1) * 128, lq * 1024:(lq + 1) * 1024],
                                    deps=ldeps)
                tok_zl[idx] = S.dma("scalar", sz4[t % 2][:],
                                    sz_d[t * 128:(t + 1) * 128, lq * 1024:(lq + 1) * 1024],
                                    deps=ldeps)
                dt_deps = [tcdw, tcdr]
                if idx > 0:
                    dt_deps.append(tok_u[idx - 1])

                def dtfn(t_, lq=lq, t=t):
                    for c in range(2):
                        ins = t_.matmul(psA[:, 2048 + c * 512:2048 + (c + 1) * 512],
                                        wdt_bf[:, t * 128:(t + 1) * 128],
                                        dtr_bf[:, lq * 1024 + c * 512:lq * 1024 + (c + 1) * 512],
                                        start=True, stop=True)
                    return ins
                tdt = S.add("tensor", dtfn, deps=dt_deps)
                u_deps = [tdt, t_bd]
                if idx > 0:
                    u_deps.append(tok_dalast[idx - 1])
                tdtb = None
                for sc in range(2):
                    so = sc * 512
                    ud = list(u_deps) if sc == 0 else []
                    if te1_prev:
                        ud.append(te1_prev)
                    tu = S.add("vector", lambda v, t=t, so=so: v.tensor_scalar_add(
                        ub4[:], psA[:, 2048 + so:2560 + so],
                        consts[:, 80 + t:80 + t + 1]), deps=ud)
                    if sc == 1:
                        tok_u[idx] = tu
                    tab = S.add("scalar", lambda s: s.activation(
                        ab4[:], ub4[:], AF.Abs), deps=[tu])
                    te1 = S.add("scalar", lambda s: s.activation(
                        eb4[:], ab4[:], AF.Exp, scale=-1.0), deps=[tab])
                    te2_deps = [te1] + ([tdtb_prev] if tdtb_prev else [])
                    te2 = S.add("scalar", lambda s: s.activation(
                        lb4[:], eb4[:], AF.Ln, bias=1.0), deps=te2_deps)
                    S.add("vector", lambda v: v.tensor_scalar_max(rl4[:], ub4[:], 0.0),
                          deps=[te1])
                    tdtb = S.add("vector", lambda v, so=so: v.tensor_add(
                        dtb[:, so:so + 512], rl4[:], lb4[:]), deps=[te2])
                    te1_prev, tdtb_prev = te1, tdtb
                dtx_deps = [tok_xl[idx], tdtb]
                if idx > 0:
                    dtx_deps.append(tok_dbx[(divmod(idx - 1, 16)[0],
                                             divmod(idx - 1, 16)[1], 15)])
                tok_dtx[idx] = S.add("vector", lambda v, t=t: v.tensor_mul(
                    dtx[:], dtb[:], xs4[t % 2][:]), deps=dtx_deps)
                for n in range(16):
                    adeps = [tdtb, t_an]
                    if n >= 2:
                        adeps.append(tok_tts[(lq, t, n - 2)])
                    tda = S.add("scalar", lambda s, t=t, n=n: s.activation(
                        dab[n % 2][:], dtb[:], AF.Exp,
                        scale=consts[:, 128 + t * 16 + n:128 + t * 16 + n + 1]),
                        deps=adeps)
                    bdeps = [tok_dtx[idx], tok_bc[(lq, n)]]
                    if n >= 2:
                        bdeps.append(tok_tts[(lq, t, n - 2)])
                    tok_dbx[(lq, t, n)] = S.add("gpsimd", lambda g, n=n: g.tensor_tensor(
                        dbx[n % 2][:], dtx[:], bcb[:, n * 1024:(n + 1) * 1024], ALU.mult),
                        deps=bdeps)
                    init = 0.0 if lq == 0 else carry[:, t * 16 + n:t * 16 + n + 1]
                    hdeps = [tda, tok_dbx[(lq, t, n)]]
                    if n >= 2:
                        hdeps.append(tok_tmp[(lq, t, n - 2)])
                    tok_tts[(lq, t, n)] = S.add(
                        "vector", lambda v, n=n, init=init: v.tensor_tensor_scan(
                            hb4[n % 2][:], dab[n % 2][:], dbx[n % 2][:], init,
                            ALU.mult, ALU.add), deps=hdeps)
                    S.add("vector", lambda v, t=t, n=n: v.tensor_copy(
                        carry[:, t * 16 + n:t * 16 + n + 1], hb4[n % 2][:, 1023:1024]))
                    teng = "gpsimd" if n % 2 == 0 else "vector"
                    tmdeps = [tok_tts[(lq, t, n)], tok_bc[(lq, 16 + n)]]
                    if n >= 2:
                        tmdeps.append(tok_ymm[(lq, t, n - 2)])
                    tok_tmp[(lq, t, n)] = S.add(teng, lambda e, n=n: e.tensor_tensor(
                        tmp4[n % 2][:], hb4[n % 2][:],
                        bcb[:, (16 + n) * 1024:(16 + n + 1) * 1024], ALU.mult),
                        deps=tmdeps)
                    ydeps = [tok_tmp[(lq, t, n)], t_if]
                    if n == 0 and idx > 0:
                        ydeps.append(tok_vb[idx - 1])

                    def yfn(t_, n=n):
                        for c in range(2):
                            ins = t_.matmul(psA[:, c * 512:(c + 1) * 512], identf[:],
                                            tmp4[n % 2][:, c * 512:(c + 1) * 512],
                                            start=(n == 0), stop=(n == 15))
                        return ins
                    tok_ymm[(lq, t, n)] = S.add("tensor", yfn, deps=ydeps)
                tok_dalast[idx] = tda
                tok_vb[idx] = None
                for sc in range(2):
                    so = sc * 512
                    tvt = S.add("vector", lambda v, t=t, so=so: v.scalar_tensor_tensor(
                        vt4[:], xs4[t % 2][:, so:so + 512], consts[:, 96 + t:96 + t + 1],
                        psA[:, so:so + 512], ALU.mult, ALU.add),
                        deps=[tok_ymm[(lq, t, 15)], t_dv])
                    vb_deps = [tok_zl[idx], tvt]
                    if sc == 0 and idx >= 2:
                        vb_deps.append(tok_vw[idx - 2])
                    tok_vb[idx] = S.add("vector", lambda v, idx=idx, t=t, so=so: v.tensor_mul(
                        vb4[idx % 2][:, so:so + 512], vt4[:], sz4[t % 2][:, so:so + 512]),
                        deps=vb_deps)
                tok_vw[idx] = S.dma(
                    "gpsimd", v_d[t * 128:(t + 1) * 128, lq * 1024:(lq + 1) * 1024],
                    vb4[idx % 2][:], deps=[tok_vb[idx]])
        S.emit_phase("scan")
        p4.close()

        p5 = ExitStack()
        wo_bf = sbt("wo_bf", [128, 16 * D], BF16, p5)
        vch = sbt("vch", [128, 16 * 512], BF16, p5)
        stile = [sbt(f"stile{j}", [128, 512], BF16, p5) for j in range(2)]

        tko = []
        for kt in range(16):
            tko.append(S.dma("gpsimd", wo_bf[:, kt * D:(kt + 1) * D],
                             woutT[kt * 128:(kt + 1) * 128, :]))
        tok_vl = {}
        tok_se = {}
        tok_sw = {}
        tok_smm = {}
        for ch in range(8):
            for kt in range(16):
                deps = [tok_vw[63]]
                if ch > 0:
                    deps.append(tok_smm[(ch - 1, 7)])
                tok_vl[(ch, kt)] = S.dma(
                    "scalar", vch[:, kt * 512:(kt + 1) * 512],
                    v_d[kt * 128:(kt + 1) * 128, ch * 512:(ch + 1) * 512], deps=deps)
            for mt in range(8):
                gi = ch * 8 + mt
                war = [tok_se[divmod(gi - 2, 8)]] if gi >= 2 else []

                def sofn(t, ch=ch, mt=mt):
                    for kt in range(16):
                        ins = t.matmul(psA[:, (mt % 2) * 512:(mt % 2) * 512 + 512],
                                       wo_bf[:, kt * D + mt * 128:kt * D + mt * 128 + 128],
                                       vch[:, kt * 512:(kt + 1) * 512],
                                       start=(kt == 0), stop=(kt == 15))
                    return ins
                tok_smm[(ch, mt)] = S.add(
                    "tensor", sofn,
                    deps=[tko[15]] + [tok_vl[(ch, kt)] for kt in range(16)] + war)
                sdeps = [tok_smm[(ch, mt)]]
                if gi >= 2:
                    sdeps.append(tok_sw[divmod(gi - 2, 8)])
                tok_se[(ch, mt)] = S.add("vector", lambda v, mt=mt, gi=gi: v.tensor_copy(
                    stile[gi % 2][:], psA[:, (mt % 2) * 512:(mt % 2) * 512 + 512]),
                    deps=sdeps)
                tok_sw[(ch, mt)] = S.dma(
                    "gpsimd", SEND[mt * 128:(mt + 1) * 128, ch * 512:(ch + 1) * 512],
                    stile[gi % 2][:], deps=[tok_se[(ch, mt)]])
        S.emit_phase("wout")
        p5.close()
        pA_outer.close()

    ct = {"ib": t_ib, "if": t_if, "o1": t_o1, "o2": t_o2, "o3": t_o3}
    return ct, [tok_sw[(7, 7)], tok_gw[31]]



def emit_bridge(nc, S, ctx, sbt, psA, psB, io, cm, ct, send_done):
    SEND = io["SEND"]
    AGG = io["AGG"]
    sfL = io["sfL"]
    srL = io["srL"]
    g0L = io["g0L"]
    jrevh = io["jrevh"]
    identbf = cm["identbf"]
    t_ib = ct["ib"]

    tcoll = S.add("gpsimd", lambda g: g.collective_compute(
        "AllGather", mybir.AluOpType.bypass,
        replica_groups=[[0, 1], [2, 3], [4, 5], [6, 7]],
        ins=[SEND.opt()], outs=[AGG.opt()]),
        deps=send_done, inc="bare")

    brs = ExitStack()
    jrev_sb = sbt("jrev_sb", [128, 128], BF16, brs)
    t_jr = S.dma("sync", jrev_sb[:], jrevh[:, :])
    mnS = sbt("br_mnS", [128, 2048], BF16, brs)
    mnS2 = sbt("br_mnS2", [128, 2048], BF16, brs)
    a0S = sbt("br_a0S", [128, 2048], BF16, brs)
    a1S = sbt("br_a1S", [128, 2048], BF16, brs)
    mnG = sbt("br_mnG", [128, 2048], BF16, brs)
    a0G = sbt("br_a0G", [128, 2048], BF16, brs)
    a1G = sbt("br_a1G", [128, 2048], BF16, brs)
    fxS = sbt("br_fxS", [128, 2048], F32, brs)
    fxS2 = sbt("br_fxS2", [128, 2048], BF16, brs)
    fxG = sbt("br_fxG", [128, 2048], F32, brs)
    fxG2 = sbt("br_fxG2", [128, 2048], BF16, brs)
    tokP = sbt("br_tokP", [128, 1024], BF16, brs)
    stg = [sbt(f"br_stg{j}", [128, 1024], BF16, brs) for j in range(3)]
    brt = {}

    def slab(base_ap, r0, c0):
        return base_ap[r0:r0 + D, c0:c0 + 256].rearrange("(kt p) t -> p kt t", p=128)

    prev = {}
    tr_ev = {}
    tr_idx = 0
    for ip in range(16):
        t0 = 256 * ip
        rc0 = L - t0 - 256
        tmn = S.dma("sync", mnS[:], slab(SEND, 0, t0),
                    deps=[tcoll, prev.get("mnS")])
        tmn2 = S.dma("sync", mnS2[:], slab(SEND, 0, rc0),
                     deps=[tcoll, prev.get("mnS2")])
        ta0 = S.dma("scalar", a0S[:], slab(AGG, 0, rc0),
                    deps=[tcoll, prev.get("a0S")])
        ta1 = S.dma("gpsimd", a1S[:], slab(AGG, D, rc0),
                    deps=[tcoll, prev.get("a1S")])
        f1 = S.add("vector", lambda v: v.tensor_sub(fxS[:], a0S[:], mnS2[:]),
                   deps=[ta0, tmn2])
        prev["mnS2"] = f1
        f2 = S.add("vector", lambda v: v.tensor_add(fxS2[:], fxS[:], a1S[:]),
                   deps=[ta1])
        prev["a0S"] = f1
        prev["a1S"] = f2
        if ip < 8:
            gmine = True
            tmg = S.dma("sync", mnG[:], slab(SEND, 0, L + t0),
                        deps=[tcoll, prev.get("mnG")])
            g2 = None
        else:
            gmine = False
            gj0 = L - t0 - 256
            tmg = S.dma("sync", mnG[:], slab(SEND, 0, L + gj0),
                        deps=[tcoll, prev.get("mnG")])
            tg0 = S.dma("scalar", a0G[:], slab(AGG, 0, L + gj0),
                        deps=[tcoll, prev.get("a0G")])
            tg1 = S.dma("gpsimd", a1G[:], slab(AGG, D, L + gj0),
                        deps=[tcoll, prev.get("a1G")])
            g1 = S.add("vector", lambda v: v.tensor_sub(fxG[:], a0G[:], mnG[:]),
                       deps=[tg0, tmg])
            g2 = S.add("vector", lambda v: v.tensor_add(fxG2[:], fxG[:], a1G[:]),
                       deps=[tg1])
            prev["a0G"] = g1
            prev["a1G"] = g2
        last_mn_use = None
        last_fx_use = None
        last_mg_use = None
        last_fg_use = None
        for st in range(2):
            r0 = t0 + st * 128
            ev_mine = []
            for kt in range(8):
                slot = (tr_idx % 2) * 128
                war = [tr_ev.get(tr_idx - 2)]
                ttr = S.add("tensor", lambda t, kt=kt, st=st, slot=slot: t.transpose(
                    psB[:, slot:slot + 128],
                    mnS[:, kt * 256 + st * 128:kt * 256 + (st + 1) * 128],
                    identbf[:]), deps=[tmn, t_ib, prev.get("stg0")] + war)
                tev = S.add("vector", lambda v, kt=kt, slot=slot: v.tensor_copy(
                    stg[0][:, kt * 128:(kt + 1) * 128], psB[:, slot:slot + 128]),
                    deps=[ttr])
                tr_ev[tr_idx] = tev
                tr_idx += 1
                ev_mine.append(tev)
                last_mn_use = ttr
            ev_part = []
            for kt in range(8):
                slot = (tr_idx % 2) * 128
                war = [tr_ev.get(tr_idx - 2)]
                ttr = S.add("tensor", lambda t, kt=kt, st=st, slot=slot: t.transpose(
                    psB[:, slot:slot + 128],
                    fxS2[:, kt * 256 + (1 - st) * 128:kt * 256 + (2 - st) * 128],
                    identbf[:]), deps=[f2, t_ib, prev.get("tokP")] + war)
                tev = S.add("vector", lambda v, kt=kt, slot=slot: v.tensor_copy(
                    tokP[:, kt * 128:(kt + 1) * 128], psB[:, slot:slot + 128]),
                    deps=[ttr])
                tr_ev[tr_idx] = tev
                tr_idx += 1
                ev_part.append(tev)
                last_fx_use = ttr
            tj1 = S.add("tensor", lambda t: t.matmul(
                psA[:, 0:512], jrev_sb[:], tokP[:, 0:512], start=True, stop=True),
                deps=ev_part + [t_jr, prev.get("psJ1")])
            tj2 = S.add("tensor", lambda t: t.matmul(
                psA[:, 512:1024], jrev_sb[:], tokP[:, 512:1024], start=True, stop=True),
                deps=[tj1, prev.get("psJ2")])
            prev["tokP"] = tj2
            evj1 = S.add("vector", lambda v: v.tensor_copy(
                stg[1][:, 0:512], psA[:, 0:512]), deps=[tj1, prev.get("stg1")])
            evj = S.add("vector", lambda v: v.tensor_copy(
                stg[1][:, 512:1024], psA[:, 512:1024]), deps=[tj2])
            prev["psJ1"] = evj1
            prev["psJ2"] = evj
            ev_g = []
            if gmine:
                for kt in range(8):
                    slot = (tr_idx % 2) * 128
                    war = [tr_ev.get(tr_idx - 2)]
                    ttr = S.add("tensor", lambda t, kt=kt, st=st, slot=slot: t.transpose(
                        psB[:, slot:slot + 128],
                        mnG[:, kt * 256 + st * 128:kt * 256 + (st + 1) * 128],
                        identbf[:]), deps=[tmg, t_ib, prev.get("stg2")] + war)
                    tev = S.add("vector", lambda v, kt=kt, slot=slot: v.tensor_copy(
                        stg[2][:, kt * 128:(kt + 1) * 128], psB[:, slot:slot + 128]),
                        deps=[ttr])
                    tr_ev[tr_idx] = tev
                    tr_idx += 1
                    ev_g.append(tev)
                    last_mg_use = ttr
                wg_deps = ev_g
            else:
                for kt in range(8):
                    slot = (tr_idx % 2) * 128
                    war = [tr_ev.get(tr_idx - 2)]
                    ttr = S.add("tensor", lambda t, kt=kt, st=st, slot=slot: t.transpose(
                        psB[:, slot:slot + 128],
                        fxG2[:, kt * 256 + (1 - st) * 128:kt * 256 + (2 - st) * 128],
                        identbf[:]), deps=[g2, t_ib, prev.get("tokP")] + war)
                    tev = S.add("vector", lambda v, kt=kt, slot=slot: v.tensor_copy(
                        tokP[:, kt * 128:(kt + 1) * 128], psB[:, slot:slot + 128]),
                        deps=[ttr])
                    tr_ev[tr_idx] = tev
                    tr_idx += 1
                    ev_g.append(tev)
                    last_fg_use = ttr
                tjg1 = S.add("tensor", lambda t: t.matmul(
                    psA[:, 0:512], jrev_sb[:], tokP[:, 0:512], start=True, stop=True),
                    deps=ev_g + [t_jr, prev.get("psJ1")])
                tjg2 = S.add("tensor", lambda t: t.matmul(
                    psA[:, 512:1024], jrev_sb[:], tokP[:, 512:1024],
                    start=True, stop=True), deps=[tjg1, prev.get("psJ2")])
                prev["tokP"] = tjg2
                evjg1 = S.add("vector", lambda v: v.tensor_copy(
                    stg[2][:, 0:512], psA[:, 0:512]), deps=[tjg1, prev.get("stg2")])
                evjg = S.add("vector", lambda v: v.tensor_copy(
                    stg[2][:, 512:1024], psA[:, 512:1024]), deps=[tjg2])
                prev["psJ1"] = evjg1
                prev["psJ2"] = evjg
                wg_deps = [evjg]
            w0 = S.dma("gpsimd", sfL[r0:r0 + 128, :], stg[0][:], deps=ev_mine)
            w1 = S.dma("gpsimd", srL[r0:r0 + 128, :], stg[1][:], deps=[evj])
            w2 = S.dma("gpsimd", g0L[r0:r0 + 128, :], stg[2][:], deps=wg_deps)
            prev["stg0"] = w0
            prev["stg1"] = w1
            prev["stg2"] = w2
            brt[2 * ip + st] = w2
        prev["mnS"] = last_mn_use
        prev["mnG"] = last_mg_use if gmine else last_mg_use or tmg
        if not gmine:
            prev["mnG"] = g1
    S.emit_phase("bridge")
    brs.close()
    return brt



def emit_B(nc, S, ctx, sbt, psA, psB, io, cm, ct, brt):
    x = io["x"]
    sfL = io["sfL"]
    srL = io["srL"]
    g0L = io["g0L"]
    attw = io["attw"]
    routT = io["routT"]
    nw = io["nw"]
    guT = io["guT"]
    dnT = io["dnT"]
    alpha = io["alpha"]
    outT = io["outT"]
    comb_d = io["comb_d"]
    x1t_d = io["x1t_d"]
    proj_d = io["proj_d"]
    xn1_d = io["xn1_d"]
    rs_d = io["rs_d"]

    QC = 256
    NQC = LH // QC

    identbf = cm["identbf"]
    identf = cm["identf"]
    onesf_row = cm["onesf_row"]
    onesf_col = cm["onesf_col"]
    onesbf_col = cm["onesbf_col"]
    t_ib = ct["ib"]
    t_if = ct["if"]
    t_o1 = ct["o1"]
    t_o2 = ct["o2"]
    t_o3 = ct["o3"]

    if True:
        onesbf_row = sbt("onesbf_row", [1, 128], BF16)
        t_o4 = S.add("gpsimd", lambda g: g.memset(onesbf_row[:], 1.0))
        sscol = sbt("sscol", [128, 32])
        rstd2col = sbt("rstd2col", [128, 32])
        acol = sbt("acol", [128, 2])
        xn1T = sbt("xn1T", [128, 8 * L], BF16)
        b12 = ExitStack()
        xn1raw = sbt("xn1raw", [128, 32 * D], BF16, b12, side="right")
        rstd2row = sbt("rstd2row", [1, LH], F32)
        tok_cpb = {}

        b1 = ExitStack()
        xt = [sbt(f"xt{j}", [128, D], F32, b1) for j in range(2)]
        sfb = [sbt(f"sfb{j}", [128, D], BF16, b1) for j in range(2)]
        srb = [sbt(f"srb{j}", [128, D], BF16, b1) for j in range(2)]
        gtb = [sbt(f"gtb{j}", [128, D], BF16, b1) for j in range(2)]
        st_ = [sbt(f"st{j}", [128, D], F32, b1) for j in range(2)]
        x1t = sbt("x1t", [128, D], F32, b1)
        xsq = [sbt(f"xsq{j}", [128, D], F32, b1) for j in range(2)]
        stat = sbt("stat", [128, 32], F32, b1)
        x1pc = sbt("x1pc", [128, 128], F32, b1)

        tok_norm = [None] * 32
        tok_tr_last = [None] * 32
        tok_rsq = [None] * 32
        tok_ssr = [None] * 32
        tok_x1w = {}
        for i in range(32):
            sb = (i % 2) * 16
            deps = [tok_norm[i - 2]] if i >= 2 else []
            bdeps = deps + [brt[i]]
            tlx = S.dma("sync", xt[i % 2][:], x[i * 128:(i + 1) * 128, :], deps=bdeps)
            tls = S.dma("sync", sfb[i % 2][:], sfL[i * 128:(i + 1) * 128, :], deps=bdeps)
            tlr = S.dma("scalar", srb[i % 2][:], srL[i * 128:(i + 1) * 128, :], deps=bdeps)
            tlg = S.dma("scalar", gtb[i % 2][:], g0L[i * 128:(i + 1) * 128, :], deps=bdeps)
            ta = S.add("vector", lambda v, i=i: v.tensor_add(
                st_[i % 2][:], sfb[i % 2][:], srb[i % 2][:]), deps=[tls, tlr])
            tb = S.add("vector", lambda v, i=i: v.tensor_mul(
                st_[i % 2][:], st_[i % 2][:], gtb[i % 2][:]), deps=[tlg])
            x1deps = [tlx]
            if i >= 1:
                x1deps.append(tok_norm[i - 1])
                if (i - 1) < 16:
                    x1deps.append(tok_x1w[(i - 1, 7)])
            tx1 = S.add("vector", lambda v, i=i: v.tensor_add(
                x1t[:], xt[i % 2][:], st_[i % 2][:]), deps=x1deps)
            tsq = S.add("scalar", lambda s, i=i: s.activation(xsq[i % 2][:], x1t[:],
                                                              AF.Square),
                        deps=[tx1] + ([tok_rsq[i - 2], tok_ssr[i - 2]]
                                      if i >= 2 else []))
            S.add("vector", lambda v, sb=sb: v.reduce_sum(
                stat[:, sb + 0:sb + 1], x1t[:], axis=AX.X), deps=[tx1])
            tok_rsq[i] = S.add("vector", lambda v, i=i, sb=sb: v.reduce_sum(
                stat[:, sb + 1:sb + 2], xsq[i % 2][:], axis=AX.X), deps=[tsq])
            S.add("vector", lambda v, sb=sb: v.tensor_scalar_mul(
                stat[:, sb + 2:sb + 3], stat[:, sb + 0:sb + 1], 1.0 / D))
            S.add("vector", lambda v, sb=sb: v.tensor_scalar_mul(
                stat[:, sb + 3:sb + 4], stat[:, sb + 1:sb + 2], 1.0 / D))
            S.add("vector", lambda v, sb=sb: v.tensor_mul(
                stat[:, sb + 4:sb + 5], stat[:, sb + 2:sb + 3], stat[:, sb + 2:sb + 3]))
            S.add("vector", lambda v, sb=sb: v.tensor_sub(
                stat[:, sb + 5:sb + 6], stat[:, sb + 3:sb + 4], stat[:, sb + 4:sb + 5]))
            tvp = S.add("vector", lambda v, sb=sb: v.tensor_scalar_add(
                stat[:, sb + 6:sb + 7], stat[:, sb + 5:sb + 6], 1e-5))
            tsqr = S.add("scalar", lambda s, sb=sb: s.activation(
                stat[:, sb + 7:sb + 8], stat[:, sb + 6:sb + 7], AF.Sqrt), deps=[tvp])
            trs = S.add("vector", lambda v, sb=sb: v.reciprocal(
                stat[:, sb + 8:sb + 9], stat[:, sb + 7:sb + 8]), deps=[tsqr])
            tnb = S.add("vector", lambda v, sb=sb: v.tensor_scalar(
                stat[:, sb + 9:sb + 10], stat[:, sb + 2:sb + 3], stat[:, sb + 8:sb + 9],
                -1.0, ALU.mult, ALU.mult))
            ndeps = [tnb]
            if i >= 1:
                ndeps.append(tok_tr_last[i - 1])
            tok_norm[i] = S.add("scalar", lambda s, i=i, sb=sb: s.activation(
                xn1raw[:, i * D:(i + 1) * D], x1t[:], AF.Identity,
                bias=stat[:, sb + 9:sb + 10], scale=stat[:, sb + 8:sb + 9]), deps=ndeps)
            tsq2 = S.add("scalar", lambda s, i=i: s.activation(
                xsq[i % 2][:], xn1raw[:, i * D:(i + 1) * D], AF.Square),
                deps=[tok_norm[i]])
            tok_ssr[i] = S.add("vector", lambda v, i=i: v.reduce_sum(
                sscol[:, i:i + 1], xsq[i % 2][:], axis=AX.X), deps=[tsq2])
            tvp2 = S.add("vector", lambda v, i=i, sb=sb: v.tensor_scalar(
                stat[:, sb + 10:sb + 11], sscol[:, i:i + 1], 1.0 / D, 1e-6,
                ALU.mult, ALU.add))
            tsq3 = S.add("scalar", lambda s, sb=sb: s.activation(
                stat[:, sb + 11:sb + 12], stat[:, sb + 10:sb + 11], AF.Sqrt),
                deps=[tvp2])
            S.add("vector", lambda v, i=i, sb=sb: v.reciprocal(
                rstd2col[:, i:i + 1], stat[:, sb + 11:sb + 12]), deps=[tsq3])
            for k in range(8):
                war = [tok_cpb[i * 8 + k - 2]] if (i * 8 + k) >= 2 else []
                ttr = S.add("tensor", lambda t, i=i, k=k: t.transpose(
                    psB[:, (k % 2) * 1024:(k % 2) * 1024 + 128],
                    xn1raw[:, i * D + k * 128:i * D + (k + 1) * 128], identbf[:]),
                    deps=[tok_norm[i], t_ib] + war)
                tok_cpb[i * 8 + k] = S.add("vector", lambda v, i=i, k=k: v.tensor_copy(
                    xn1T[:, k * L + i * 128:k * L + (i + 1) * 128],
                    psB[:, (k % 2) * 1024:(k % 2) * 1024 + 128]), deps=[ttr])
                tok_tr_last[i] = ttr
            if i < 16:
                for k in range(8):
                    slot = (k % 2) * 512
                    war = [tok_x1w[(i, k - 2)]] if k >= 2 else (
                        [tok_x1w[(i - 1, 6)], tok_x1w[(i - 1, 7)]] if i >= 1 else [])
                    ttf = S.add("tensor", lambda t, i=i, k=k, slot=slot: t.transpose(
                        psA[:, slot:slot + 128], x1t[:, k * 128:(k + 1) * 128],
                        identf[:]), deps=[tx1, t_if] + war)
                    cf_deps = [ttf]
                    if k >= 1:
                        cf_deps.append(tok_x1w[(i, k - 1)])
                    elif i >= 1:
                        cf_deps.append(tok_x1w[(i - 1, 7)])
                    tcf = S.add("vector", lambda v, slot=slot: v.tensor_copy(
                        x1pc[:], psA[:, slot:slot + 128]), deps=cf_deps)
                    tok_x1w[(i, k)] = S.dma(
                        "gpsimd", x1t_d[k * 128:(k + 1) * 128, i * 128:(i + 1) * 128],
                        x1pc[:], deps=[tcf])
        trt2 = S.add("tensor", lambda t: t.transpose(
            psA[0:32, 1024:1152], rstd2col[:, 0:32], identf[:]),
            deps=[S.last_comp["vector"], t_if])
        trc2 = S.add("vector", lambda v: v.tensor_copy(x1pc[0:32, 0:128],
                                                       psA[0:32, 1024:1152]),
                     deps=[trt2, tok_x1w[(15, 7)]])
        trq1 = S.dma("gpsimd", rs_d[:, :], x1pc[0:32, 0:128], deps=[trc2])
        trq2 = S.dma("gpsimd", rstd2row[0:1, :],
                     rs_d[0:16, :].rearrange("t p -> (t p)").unsqueeze(0), deps=[trq1])
        for kq in range(8):
            S.dma("gpsimd", xn1_d[:, kq * L:(kq + 1) * L],
                  xn1T[:, kq * L:(kq + 1) * L], deps=[tok_cpb[255]])
        tok_xnd = S.ops["gpsimd"][-1]
        tok_xnd = ("gpsimd", tok_xnd)
        S.emit_phase("b1")
        b1.close()

        b2 = ExitStack()
        attw_bf = sbt("attw_bf", [128, 8 * D], BF16, b2)
        ebuf = sbt("ebuf", [128, 32 * QC], BF16, b2)
        obuf = sbt("obuf", [128, 8 * QC], BF16, b2)
        sqh = sbt("sqh", [64, 2048], BF16, b2)
        ssq_row = sbt("ssq_row", [1, L], BF16, b2)
        m_row = sbt("m_row", [1, L], BF16, b2)
        kmax = sbt("kmax", [1, 2], F32, b2)
        mbp = sbt("mbp", [128, QC], F32, b2)
        zrow = sbt("zrow", [1, QC], F32, b2)
        zrb = sbt("zrb", [128, QC], F32, b2)
        sc_f = sbt("sc_f", [128, QC], F32, b2)
        prj = sbt("prj", [128, QC], F32, b2)

        tok_pw = {}
        prev_exp = {}
        prev_ev = {}
        prev_olast = None
        prev_attw = None
        prev_head_tp = None
        for h in range(16):
            pt, r0 = h // 2, (h % 2) * 64
            ta_ld = []
            for ptw in range(8):
                adeps = [prev_attw] if prev_attw else []
                if ptw == 0 and prev_head_tp is not None:
                    adeps.append(prev_head_tp)
                ta_ld.append(S.dma(
                    "gpsimd", attw_bf[:, ptw * D:(ptw + 1) * D],
                    attw[h * D + ptw * 128:h * D + (ptw + 1) * 128, :], deps=adeps))
            tsq_l = []
            for hf in range(2):
                sq_deps = [tok_cpb[255]]
                tsq_l.append(S.add("vector", lambda v, pt=pt, r0=r0, hf=hf: v.tensor_mul(
                    sqh[:, 0:2048],
                    xn1T[r0:r0 + 64, pt * L + hf * 2048:pt * L + (hf + 1) * 2048],
                    xn1T[r0:r0 + 64, pt * L + hf * 2048:pt * L + (hf + 1) * 2048]),
                    deps=sq_deps))
            tzp = None
            for chq in range(8):
                tmm = S.add("tensor", lambda t, chq=chq: t.matmul(
                    psA[0:1, 1536:2048], onesbf_col[0:64, 0:1],
                    sqh[:, (chq % 4) * 512:(chq % 4) * 512 + 512],
                    start=True, stop=True),
                    deps=[tsq_l[chq // 4], t_o3] + ([tzp] if tzp else []))
                tzp = S.add("vector", lambda v, chq=chq: v.tensor_copy(
                    ssq_row[0:1, chq * 512:(chq + 1) * 512], psA[0:1, 1536:2048]),
                    deps=[tmm])
            tqn = S.add("scalar", lambda s: s.activation(m_row[0:1, :], ssq_row[0:1, :],
                                                         AF.Sqrt), deps=[tzp])
            tmx = S.add("vector", lambda v: v.reduce_max(kmax[0:1, 0:1], m_row[0:1, :],
                                                         axis=AX.X), deps=[tqn])
            tsc = S.add("vector", lambda v: v.tensor_scalar(
                kmax[0:1, 1:2], kmax[0:1, 0:1], 0.1275, 1.0, ALU.mult, ALU.add))
            tmr = S.add("vector", lambda v: v.tensor_single_scalar(
                m_row[0:1, :], m_row[0:1, :], kmax[0:1, 1:2], ALU.mult), deps=[tsc])
            for qc in range(NQC):
                tmb = S.add("tensor", lambda t, qc=qc: t.matmul(
                    psA[:, 1536:1536 + QC], onesbf_row[0:1, :],
                    m_row[0:1, qc * QC:(qc + 1) * QC], start=True, stop=True),
                    deps=[tmr, t_o4] + ([tok_pw[(h, qc - 1, 7)]] if qc >= 1 else
                                        ([tok_pw[(h - 1, NQC - 1, 7)]] if h >= 1 else [])))
                tmbe = S.add("vector", lambda v: v.tensor_copy(mbp[:], psA[:, 1536:1536 + QC]),
                             deps=[tmb])
                tok_exp = {}
                for stt_ in range(32):
                    slot = (stt_ % 2) * 512
                    war = [prev_ev.get(stt_ % 2)]
                    if stt_ == 0:
                        war.append(prev_olast)
                    tsmm = S.add("tensor", lambda t, pt=pt, r0=r0, stt_=stt_, qc=qc, slot=slot:
                                 t.matmul(psA[:, slot:slot + QC],
                                          xn1T[r0:r0 + 64, pt * L + stt_ * 128:pt * L + (stt_ + 1) * 128],
                                          xn1T[r0:r0 + 64, pt * L + qc * QC:pt * L + (qc + 1) * QC],
                                          start=True, stop=True),
                                 deps=[tok_cpb[255]] + war)
                    tev = S.add("vector", lambda v, slot=slot: v.scalar_tensor_tensor(
                        sc_f[:], psA[:, slot:slot + QC], 0.125, mbp[:],
                        ALU.mult, ALU.subtract),
                        deps=[tsmm, tmbe] + ([prev_exp[0]] if 0 in prev_exp else []))
                    tok_exp[stt_] = S.add("scalar", lambda s, stt_=stt_: s.activation(
                        ebuf[:, stt_ * QC:(stt_ + 1) * QC], sc_f[:], AF.Exp),
                        deps=[tev])
                    prev_exp[0] = tok_exp[stt_]
                    prev_ev[stt_ % 2] = tev
                tz = None
                for stt_ in range(32):
                    tz = S.add("tensor", lambda t, stt_=stt_: t.matmul(
                        psA[0:1, 1536:1536 + QC], onesbf_col[:, 0:1],
                        ebuf[:, stt_ * QC:(stt_ + 1) * QC],
                        start=(stt_ == 0), stop=(stt_ == 31)),
                        deps=[tok_exp[stt_], t_o3] + ([tmbe] if stt_ == 0 else []))
                tze = S.add("vector", lambda v: v.tensor_copy(zrow[0:1, :],
                                                              psA[0:1, 1536:1536 + QC]),
                            deps=[tz])
                trz = S.add("vector", lambda v: v.reciprocal(zrow[0:1, :], zrow[0:1, :]))
                tzb = S.add("tensor", lambda t: t.matmul(
                    psA[:, 2048:2048 + QC], onesf_row[0:1, :], zrow[0:1, :],
                    start=True, stop=True), deps=[trz, t_o1])
                tzbe = S.add("vector", lambda v: v.tensor_copy(zrb[:], psA[:, 2048:2048 + QC]),
                             deps=[tzb])
                tprev_o = None
                for cb in range(8):
                    to = None
                    for stt_ in range(32):
                        to = S.add("tensor", lambda t, cb=cb, stt_=stt_: t.matmul(
                            psA[:, 1024:1024 + QC],
                            xn1raw[:, stt_ * D + cb * 128:stt_ * D + cb * 128 + 128],
                            ebuf[:, stt_ * QC:(stt_ + 1) * QC],
                            start=(stt_ == 0), stop=(stt_ == 31)),
                            deps=[tok_exp[31]] + ([tprev_o] if (stt_ == 0 and tprev_o) else []))
                    tprev_o = S.add("vector", lambda v, cb=cb: v.tensor_mul(
                        obuf[:, cb * QC:(cb + 1) * QC], psA[:, 1024:1024 + QC], zrb[:]),
                        deps=[to, tzbe])
                    prev_olast = to
                for mt in range(8):
                    pslot = 2560 if mt % 2 == 0 else 1536
                    tp = None
                    for cb in range(8):
                        tp = S.add("tensor", lambda t, mt=mt, cb=cb, pslot=pslot: t.matmul(
                            psA[:, pslot:pslot + QC],
                            attw_bf[:, cb * D + mt * 128:cb * D + mt * 128 + 128],
                            obuf[:, cb * QC:(cb + 1) * QC],
                            start=(cb == 0), stop=(cb == 7)),
                            deps=[tprev_o, ta_ld[cb]] +
                                 ([tok_pw[(h, qc, mt - 2)]] if mt >= 2 else
                                  ([tok_pw[(h, qc - 1, mt + 6)]] if qc >= 1 else
                                   ([tok_pw[(h - 1, NQC - 1, mt + 6)]] if (h >= 1 and mt >= 0 and mt < 2) else []))))
                    prj_war = []
                    if mt >= 1:
                        prj_war.append(tok_pw[(h, qc, mt - 1)])
                    elif qc >= 1:
                        prj_war.append(tok_pw[(h, qc - 1, 7)])
                    elif h >= 1:
                        prj_war.append(tok_pw[(h - 1, NQC - 1, 7)])
                    if h == 0:
                        tv = S.add(
                            "vector", lambda v, pslot=pslot: v.tensor_copy(
                                prj[:], psA[:, pslot:pslot + QC]), deps=[tp] + prj_war)
                    else:
                        tld = S.dma("sync", prj[:],
                                    proj_d[mt * 128:(mt + 1) * 128, qc * QC:(qc + 1) * QC],
                                    deps=[tok_pw[(h - 1, qc, mt)]] + prj_war)
                        tv = S.add(
                            "vector", lambda v, pslot=pslot: v.tensor_add(
                                prj[:], prj[:], psA[:, pslot:pslot + QC]),
                            deps=[tp, tld])
                    tok_pw[(h, qc, mt)] = S.dma(
                        "scalar", proj_d[mt * 128:(mt + 1) * 128, qc * QC:(qc + 1) * QC],
                        prj[:], deps=[tv])
                    prev_head_tp = tp
            prev_attw = ta_ld[7]
        S.emit_phase("b2")
        b2.close()
        b12.close()

        b3 = ExitStack()
        moeacc = sbt("moeacc", [128, 8 * LH], F32, b3, side="right")
        nwcol = sbt("nwcol", [128, 8], F32, b3)
        combrow = sbt("combrow", [8, L], BF16, b3)
        b3a = ExitStack()
        rw_bf = sbt("rw_bf", [128, 8 * NEXP], BF16, b3a)
        logT = sbt("logT", [8, L], F32, b3a)
        ltile = sbt("ltile", [128, 8], F32, b3a)
        top8 = sbt("top8", [128, 8], F32, b3a)
        l2t = sbt("l2t", [128, 8], F32, b3a)
        l3t = sbt("l3t", [128, 8], F32, b3a)
        msk = sbt("msk", [128, 8], F32, b3a)
        wcol = sbt("wcol", [128, 4], F32, b3a)
        combt = sbt("combt", [128, 8], F32, b3a)
        mrv = sbt("mrv", [128, 8], F32, b3a)


        tnw = S.dma("sync", nwcol[:, :], nw.rearrange("(t p) o -> p t o", p=128),
                    slow=True)
        trt = S.dma("gpsimd", rw_bf[:, :], routT.rearrange("(t p) e -> p t e", p=128))
        tfold = []
        for kt in range(8):
            tfold.append(S.add("vector", lambda v, kt=kt: v.tensor_single_scalar(
                rw_bf[:, kt * NEXP:(kt + 1) * NEXP], rw_bf[:, kt * NEXP:(kt + 1) * NEXP],
                nwcol[:, kt:kt + 1], ALU.mult), deps=[tnw, trt]))
        tle = None
        for chl in range(8):
            def lmfn(t, chl=chl):
                for kt in range(8):
                    ins = t.matmul(psA[0:8, 0:512],
                                   rw_bf[:, kt * NEXP:(kt + 1) * NEXP],
                                   xn1T[:, kt * L + chl * 512:kt * L + (chl + 1) * 512],
                                   start=(kt == 0), stop=(kt == 7))
                return ins
            tlm = S.add("tensor", lmfn, deps=tfold + ([tle] if tle else []))
            tle = S.add("vector", lambda v, chl=chl: v.tensor_copy(
                logT[:, chl * 512:(chl + 1) * 512], psA[0:8, 0:512]), deps=[tlm])
        tok_cr = {}
        for i in range(32):
            ttr = S.add("tensor", lambda t, i=i: t.transpose(
                psA[0:128, 512:520], logT[0:8, i * 128:(i + 1) * 128],
                identf[0:8, 0:8]),
                deps=[tle, t_if] + ([tok_cr[i - 1]] if i >= 1 else []))
            tlt = S.add("vector", lambda v, i=i: v.tensor_single_scalar(
                ltile[:], psA[:, 512:640][:, 0:8], rstd2col[:, i:i + 1], ALU.mult),
                deps=[ttr])
            t8 = S.add("vector", lambda v: v.max(top8[:, 0:8], ltile[:]))
            if i == 0:
                tok_mrv = S.add("gpsimd", lambda g: g.memset(mrv[:], 1e30), deps=[tle])
            S.add("vector", lambda v: v.tensor_copy(mrv[:, 0:1], top8[:, 0:1]),
                  deps=[tok_mrv] if i == 0 else [])
            tm1 = S.add("vector", lambda v: v.match_replace(
                l2t[:], mrv[:, 0:8], ltile[:], -1e30))
            S.add("vector", lambda v: v.tensor_copy(mrv[:, 0:1], top8[:, 1:2]))
            tm2 = S.add("vector", lambda v: v.match_replace(
                l3t[:], mrv[:, 0:8], l2t[:], -1e30))
            td = S.add("vector", lambda v: v.tensor_sub(
                wcol[:, 0:1], top8[:, 0:1], top8[:, 1:2]))
            tsg = S.add("scalar", lambda s: s.activation(wcol[:, 1:2], wcol[:, 0:1],
                                                         AF.Sigmoid), deps=[td])
            tw2 = S.add("vector", lambda v: v.tensor_scalar(
                wcol[:, 2:3], wcol[:, 1:2], -1.0, 1.0, ALU.mult, ALU.add), deps=[tsg])
            tk1 = S.add("vector", lambda v: v.tensor_tensor(
                msk[:], ltile[:], l2t[:], ALU.not_equal))
            tc1 = S.add("vector", lambda v: v.tensor_single_scalar(
                combt[:], msk[:], wcol[:, 1:2], ALU.mult))
            tk2 = S.add("vector", lambda v: v.tensor_tensor(
                msk[:], l2t[:], l3t[:], ALU.not_equal))
            tc2 = S.add("vector", lambda v: v.scalar_tensor_tensor(
                combt[:], msk[:], wcol[:, 2:3], combt[:], ALU.mult, ALU.add))
            tc3 = S.add("vector", lambda v, i=i: v.tensor_single_scalar(
                combt[:], combt[:], rstd2col[:, i:i + 1], ALU.mult))
            ttb = S.add("tensor", lambda t, i=i: t.transpose(
                psA[0:8, 512:640], combt[:, 0:8], identf[:]), deps=[tc3, t_if])
            tok_cr[i] = S.add("vector", lambda v, i=i: v.tensor_copy(
                combrow[0:8, i * 128:(i + 1) * 128], psA[0:8, 512:640]), deps=[ttb])
        tcw = S.dma("sync", comb_d[:, :], combrow[:, :], deps=[tok_cr[31]])
        b3a.close()
        tal = S.dma("sync", acol[0:1, 0:1], alpha[0:1, 0:1])
        tac1 = S.add("vector", lambda v: v.tensor_scalar_min(acol[0:1, 0:1], acol[0:1, 0:1], 2.0),
                     deps=[tal])
        tac2 = S.add("vector", lambda v: v.tensor_scalar_max(acol[0:1, 0:1], acol[0:1, 0:1], -0.5))
        tab = S.add("tensor", lambda t: t.matmul(psA[:, 640:641], onesf_row[0:1, :],
                                                 acol[0:1, 0:1], start=True, stop=True),
                    deps=[tac2, t_o1])
        tabe = S.add("vector", lambda v: v.tensor_copy(acol[:, 1:2], psA[:, 640:641]),
                     deps=[tab])

        gslice = [sbt(f"gsl{j}", [128, 8 * 128], BF16, b3) for j in range(2)]
        dslice = [sbt(f"dsl{j}", [128, 22 * 128], BF16, b3) for j in range(2)]
        hidb = sbt("hidb", [128, 22 * 512], BF16, b3)
        gsil = sbt("gsil", [128, 512], BF16, b3)
        tmpf = sbt("tmpf", [128, 512], F32, b3)
        cbb = sbt("cbb", [128, 512], BF16, b3)
        rsb4 = [sbt(f"rsb{j}", [128, 512], F32, b3) for j in range(4)]
        tok_fold = None
        for ptf in range(8):
            tok_fold = S.add("vector", lambda v, ptf=ptf: v.tensor_single_scalar(
                xn1T[:, ptf * L:(ptf + 1) * L], xn1T[:, ptf * L:(ptf + 1) * L],
                nwcol[:, ptf:ptf + 1], ALU.mult), deps=[tnw, tok_xnd, tle])
        tok_ma = {}
        prev_gl = {}
        prev_dl = {}
        tok_rsbe = {}
        for e in range(NEXP):
            for qc in range(4):
                war_ma = ([tok_ma[(e, qc - 1, 7)]] if qc >= 1 else
                          ([tok_ma[(e - 1, 3, 7)]] if e >= 1 else []))
                tcbe = S.dma("sync", cbb[:, :],
                             comb_d[e:e + 1, qc * 512:(qc + 1) * 512].partition_broadcast(128),
                             deps=[tcw] + war_ma)
                if e == 0:
                    trsb = S.add("tensor", lambda t, qc=qc: t.matmul(
                        psA[:, 2048:2560], onesf_row[0:1, :],
                        rstd2row[0:1, qc * 512:(qc + 1) * 512], start=True, stop=True),
                        deps=[trq2, t_o1] + war_ma +
                             ([tok_rsbe[qc - 1]] if qc >= 1 else []))
                    tok_rsbe[qc] = S.add("vector", lambda v: v.tensor_copy(
                        rsb4[qc][:], psA[:, 2048:2560]), deps=[trsb])
                for mh in range(22):
                    for half, mtt in ((0, mh), (1, 22 + mh)):
                        gdeps = []
                        key = (qc * 44 + mtt) % 2
                        if (e, mtt) != (0, 0) or qc > 0:
                            pass
                        war = [prev_gl[key]] if key in prev_gl else [tcw, tle]
                        tgl = S.dma("gpsimd", gslice[key][:, :],
                                    guT[e, :, mtt * 128:(mtt + 1) * 128].rearrange(
                                        "(kt p) m -> p kt m", p=128), deps=war)
                        def gufn(t, key=key, qc=qc):
                            for kt in range(8):
                                ins = t.matmul(
                                    psA[:, (key) * 512:(key) * 512 + 512],
                                    gslice[key][:, kt * 128:(kt + 1) * 128],
                                    xn1T[:, kt * L + qc * 512:kt * L + (qc + 1) * 512],
                                    start=(kt == 0), stop=(kt == 7))
                            return ins
                        tgm = S.add("tensor", gufn, deps=[tgl, tok_fold])
                        prev_gl[key] = tgm
                        if half == 0:
                            tsc2 = S.add("vector", lambda v, key=key, qc=qc: v.tensor_mul(
                                tmpf[:], psA[:, key * 512:key * 512 + 512], rsb4[qc][:]),
                                deps=[tgm, tok_rsbe[qc]])
                            tgsil = S.add("scalar", lambda s: s.activation(
                                gsil[:], tmpf[:], AF.Silu), deps=[tsc2])
                        else:
                            tup = S.add("vector", lambda v, key=key: v.tensor_mul(
                                tmpf[:], psA[:, key * 512:key * 512 + 512], cbb[:]),
                                deps=[tgm, tcbe, tgsil])
                            thid = S.add("vector", lambda v, mh=mh: v.tensor_mul(
                                hidb[:, mh * 512:(mh + 1) * 512], tmpf[:], gsil[:]),
                                deps=[tup, tgsil])
                for mtd in range(8):
                    key = mtd % 2
                    war = [prev_dl[key]] if key in prev_dl else [tcw, tle]
                    tdl = S.dma("gpsimd", dslice[key][:, :],
                                dnT[e, :, mtd * 128:(mtd + 1) * 128].rearrange(
                                    "(kt p) m -> p kt m", p=128), deps=war)
                    def dnfn(t, key=key, mtd=mtd):
                        for kt in range(22):
                            ins = t.matmul(
                                psA[:, 1024 + key * 512:1024 + key * 512 + 512],
                                dslice[key][:, kt * 128:(kt + 1) * 128],
                                hidb[:, kt * 512:(kt + 1) * 512],
                                start=(kt == 0), stop=(kt == 21))
                        return ins
                    tdm = S.add("tensor", dnfn,
                                deps=[tdl, thid] + ([tok_ma[(e, qc, mtd - 2)]]
                                                    if mtd >= 2 else
                                                    ([tok_ma[(e, qc - 1, mtd + 6)]] if qc >= 1 else
                                                     ([tok_ma[(e - 1, 3, mtd + 6)]] if e >= 1 else []))))
                    prev_dl[key] = tdm
                    if e == 0:
                        tok_ma[(e, qc, mtd)] = S.add(
                            "vector", lambda v, qc=qc, mtd=mtd, key=key: v.tensor_copy(
                                moeacc[:, mtd * LH + qc * 512:mtd * LH + (qc + 1) * 512],
                                psA[:, 1024 + key * 512:1024 + key * 512 + 512]),
                            deps=[tdm])
                    else:
                        tok_ma[(e, qc, mtd)] = S.add(
                            "vector", lambda v, qc=qc, mtd=mtd, key=key: v.tensor_add(
                                moeacc[:, mtd * LH + qc * 512:mtd * LH + (qc + 1) * 512],
                                moeacc[:, mtd * LH + qc * 512:mtd * LH + (qc + 1) * 512],
                                psA[:, 1024 + key * 512:1024 + key * 512 + 512]),
                            deps=[tdm])
        S.emit_phase("b3")
        b3.close()

        b4 = ExitStack()
        x1l = sbt("x1l", [128, 2048], F32, b4)
        s1l = sbt("s1l", [128, 2048], F32, b4)
        xnl = sbt("xnl", [128, 2048], BF16, b4)
        g1l = sbt("g1l", [128, 2048], BF16, b4)
        otl = sbt("otl", [128, 2048], F32, b4)
        tprev = None
        for ptv in range(8):
            dl = [tprev] if tprev else []
            tl1 = S.dma("sync", x1l[:], x1t_d[ptv * 128:(ptv + 1) * 128, :],
                        deps=[tok_x1w[(15, 7)]] + dl)
            tl2 = S.dma("scalar", s1l[:], proj_d[ptv * 128:(ptv + 1) * 128, :],
                        deps=[tok_pw[(15, NQC - 1, 7)]] + dl)
            tl3 = S.dma("sync", xnl[:], xn1_d[:, ptv * L:ptv * L + 2048],
                        deps=[tok_xnd] + dl)
            tg1 = S.add("vector", lambda v, ptv=ptv: v.scalar_tensor_tensor(
                otl[:], moeacc[:, ptv * LH:(ptv + 1) * LH], acol[:, 1:2],
                xnl[:], ALU.mult, ALU.add),
                deps=[tok_ma[(7, 3, ptv)], tabe, tl3])
            tg2 = S.add("scalar", lambda s: s.activation(g1l[:], otl[:], AF.Silu),
                        deps=[tg1])
            tg3 = S.add("vector", lambda v: v.tensor_mul(otl[:], s1l[:], g1l[:]),
                        deps=[tg2, tl2])
            tg4 = S.add("vector", lambda v: v.tensor_add(otl[:], otl[:], x1l[:]),
                        deps=[tl1])
            tprev = S.dma("gpsimd", outT[ptv * 128:(ptv + 1) * 128, :], otl[:],
                          deps=[tg4])
        S.add("gpsimd", lambda g: g.nop(), deps=[tprev])
        S.emit_phase("b4")
        b4.close()



def build_fused(paranoid=False, debug=False, paranoid_phases=None):
    nc = bass.Bass("TRN2", target_bir_lowering=False)
    io = {}
    DK = "ExternalOutput" if debug else "Internal"

    def din(name, shape, dt=F32):
        io[name] = nc.dram_tensor(name, shape, dt, kind="ExternalInput").ap()

    def dint(name, shape, dt=F32, dk="Internal"):
        io[name] = nc.dram_tensor(name, shape, dt, kind=dk).ap()

    din("x", [L, D])
    din("winT", [D, 2 * DI])
    din("wxT", [DI, 96])
    din("wdtT", [RK, DI])
    din("convw", [DI, 4])
    din("convb", [DI, 1])
    din("alog", [DI, NST])
    din("bdt", [DI, 1])
    din("dvec", [DI, 1])
    din("woutT", [DI, D])
    din("wupT", [D, DFF])
    din("wdnT", [DFF, D])
    din("idbf", [128, 128], BF16)
    din("idf32", [128, 128], F32)
    din("jrevh", [128, 128], BF16)
    din("attw", [NHEAD * D, D])
    din("routT", [D, NEXP])
    din("nw", [D, 1])
    din("guT", [NEXP, D, 2 * HID])
    din("dnT", [NEXP, HID, D])
    din("alpha", [1, 1])
    io["outT"] = nc.dram_tensor("outT", [D, LH], F32, kind="ExternalOutput").ap()
    dint("xssm_d", [DI, L], BF16)
    dint("sz_d", [DI, L], BF16)
    dint("v_d", [DI, L], BF16)
    dint("bc_d", [2 * NST, L], F32)
    dint("sq_d", [32, 128], F32)
    dint("h_d", [DFF, LH], BF16, DK)
    dint("SEND", [D, L + LH], BF16)
    dint("AGG", [2 * D, L + LH], BF16)
    dint("sfL", [L, D], BF16, DK)
    dint("srL", [L, D], BF16, DK)
    dint("g0L", [L, D], BF16, DK)
    dint("comb_d", [NEXP, L], BF16)
    dint("x1t_d", [D, LH], F32, DK)
    dint("proj_d", [D, LH], F32, DK)
    dint("xn1_d", [128, 8 * L], BF16, DK)
    dint("rs_d", [32, 128], F32)

    ctx = ExitStack()
    with ctx:
        def sbt(name, shape, dt=F32, st=None, side=None):
            if side:
                return (st or ctx).enter_context(nc.sbuf_tensor(name, shape, dt, side=side))
            return (st or ctx).enter_context(nc.sbuf_tensor(name, shape, dt))

        S = Sched(nc, ctx, paranoid=paranoid, paranoid_phases=paranoid_phases)
        psA = ctx.enter_context(nc.psum_tensor("psA", [128, 3072], F32))
        psB = ctx.enter_context(nc.psum_tensor("psB", [128, 2048], BF16))

        cm = {
            "identbf": sbt("identbf", [128, 128], BF16),
            "identf": sbt("identf", [128, 128], F32),
            "onesf_row": sbt("onesf_row", [1, 128]),
            "onesf_col": sbt("onesf_col", [128, 1]),
            "onesbf_col": sbt("onesbf_col", [128, 1], BF16),
        }
        def sbtB(name, shape, dt=F32, st=None, side=None):
            return sbt("B" + name, shape, dt, st, side)

        ct, send_done = emit_A(nc, S, ctx, sbt, psA, psB, io, cm)
        brt = emit_bridge(nc, S, ctx, sbt, psA, psB, io, cm, ct, send_done)
        emit_B(nc, S, ctx, sbtB, psA, psB, io, cm, ct, brt)
    return nc



_CACHE = {}
PARANOID = False
PARANOID_PHASES = {5, 8}


def _prep_A_inputs(inp):
    c = np.ascontiguousarray
    out = {
        "winT": c(np.asarray(inp["L0_attn_W_in"]).T),
        "woutT": c(np.asarray(inp["L0_attn_W_out"]).T),
        "wupT": c(np.asarray(inp["L0_ffn_w_up"]).T),
        "wdnT": c(np.asarray(inp["L0_ffn_w_down"]).T),
        "idbf": np.eye(128, dtype=BF),
        "idf32": np.eye(128, dtype=np.float32),
    }
    return out


def _prep_A_dir(inp, d):
    c = np.ascontiguousarray
    sfx = "fwd" if d == 0 else "rev"
    return {
        "wxT": c(np.asarray(inp[f"L0_attn_Wx_{sfx}"]).T),
        "wdtT": c(np.asarray(inp[f"L0_attn_Wdt_{sfx}"]).T),
        "convw": c(np.asarray(inp[f"L0_attn_conv_w_{sfx}"])),
        "convb": c(np.asarray(inp[f"L0_attn_conv_b_{sfx}"]).reshape(DI, 1)),
        "alog": c(np.asarray(inp[f"L0_attn_A_log_{sfx}"])),
        "bdt": c(np.asarray(inp[f"L0_attn_bdt_{sfx}"]).reshape(DI, 1)),
        "dvec": c(np.asarray(inp[f"L0_attn_D_{sfx}"]).reshape(DI, 1)),
    }


def _prep_B_shared(inp):
    c = np.ascontiguousarray
    gu = np.asarray(inp["L1_ffn_gate_up_weight"])
    dn = np.asarray(inp["L1_ffn_down_weight"])
    return {
        "attw": c(np.asarray(inp["L1_attn_weight"])),
        "routT": c(np.asarray(inp["L1_ffn_router_weight"]).T),
        "nw": c(np.asarray(inp["L1_ffn_norm_weight"]).reshape(D, 1)),
        "guT": c(gu.transpose(0, 2, 1)),
        "dnT": c(dn.transpose(0, 2, 1)),
        "alpha": c(np.asarray(inp["L1_ffn_residual_alpha"], dtype=np.float32).reshape(1, 1)),
        "idbf": np.eye(128, dtype=BF),
        "idf32": np.eye(128, dtype=np.float32),
    }


def kernel(**inputs):
    if "F" not in _CACHE:
        _CACHE["F"] = build_fused(paranoid=PARANOID, paranoid_phases=PARANOID_PHASES)
    ncF = _CACHE["F"]
    sharedA = _prep_A_inputs(inputs)
    sharedB = _prep_B_shared(inputs)
    dirs = [_prep_A_dir(inputs, 0), _prep_A_dir(inputs, 1)]
    jrev = np.ascontiguousarray(np.eye(128, dtype=BF)[::-1])
    xfull = np.asarray(inputs["L0_attn_hidden_states"])
    in_maps = []
    for c in range(8):
        b, d = c // 2, c % 2
        m = dict(sharedA)
        m.update(sharedB)
        m.update(dirs[d])
        m["jrevh"] = jrev
        xb = xfull[b] if d == 0 else xfull[b][::-1]
        m["x"] = np.ascontiguousarray(xb)
        in_maps.append(m)
    res = run_bass_kernel_spmd(ncF, in_maps, core_ids=list(range(8)))
    out = np.empty((NB, L, D), np.float32)
    for c in range(8):
        b, d = c // 2, c % 2
        o = res.results[c]["outT"].T
        if d == 0:
            out[b, :LH] = o
        else:
            out[b, LH:] = o[::-1]
    return out
